# version 01
# baseline (speedup 1.0000x reference)
import numpy as np
import ml_dtypes
import contextlib
import concourse.bass as bass
import concourse.mybir as mybir
from concourse.bass_utils import run_bass_kernel_spmd

F32 = mybir.dt.float32
BF16 = mybir.dt.bfloat16
AF = mybir.ActivationFunctionType
ALU = mybir.AluOpType
BF = ml_dtypes.bfloat16

D_MODEL = 1024
D_INNER = 2048
CONV_DIM = 2304
NHEADS = 32
DCONV = 4
DIN = 4384
Q = 128
NEG = -1e30
LNEPS = 1e-5
RMSEPS = 16e-5


class Sem:
    def __init__(self, st, nc, name):
        self.h = st.enter_context(nc.semaphore(name))
        self.n = 0

    def bump(self, instr, k=1):
        self.n += k
        instr.then_inc(self.h, k)
        return self.n

    def dma(self, eng, out, in_):
        eng.wait_ge(self.h, self.n)
        return self.bump(eng.dma_start(out=out, in_=in_), 16)


def build_nc(n_seq, n_chunks, races=False):
    nc = bass.Bass("TRN2", target_bir_lowering=False, detect_race_conditions=races)

    u_d = nc.dram_tensor("u", [n_seq, n_chunks * Q, D_MODEL], F32, kind="ExternalInput").ap()
    out_d = nc.dram_tensor("out", [n_seq, n_chunks * Q, D_MODEL], F32, kind="ExternalOutput").ap()
    winT_d = nc.dram_tensor("winT", [D_MODEL, DIN], F32, kind="ExternalInput").ap()
    woutT_d = nc.dram_tensor("woutT", [D_INNER, D_MODEL], F32, kind="ExternalInput").ap()
    nw_d = nc.dram_tensor("nw", [128, 16], F32, kind="ExternalInput").ap()
    convwT_d = nc.dram_tensor("convwT", [DCONV, CONV_DIM], F32, kind="ExternalInput").ap()
    convb_d = nc.dram_tensor("convb", [1, CONV_DIM], F32, kind="ExternalInput").ap()
    dtb_d = nc.dram_tensor("dtb", [1, NHEADS], F32, kind="ExternalInput").ap()
    alog_d = nc.dram_tensor("alog", [NHEADS, 1], F32, kind="ExternalInput").ap()
    dpar_d = nc.dram_tensor("dpar", [1, NHEADS], F32, kind="ExternalInput").ap()
    fbias_d = nc.dram_tensor("fbias", [1, 1024], F32, kind="ExternalInput").ap()
    z1_d = nc.dram_tensor("z1", [32, 4, 32], F32, kind="ExternalInput").ap()
    z2_d = nc.dram_tensor("z2", [16, 4, 16], F32, kind="ExternalInput").ap()
    z1i_d = nc.dram_tensor("z1i", [32, 4, 32], F32, kind="ExternalInput").ap()
    z2i_d = nc.dram_tensor("z2i", [16, 4, 16], F32, kind="ExternalInput").ap()
    identb_d = nc.dram_tensor("identb", [128, 128], BF16, kind="ExternalInput").ap()
    identf_d = nc.dram_tensor("identf", [128, 128], F32, kind="ExternalInput").ap()
    mask_d = nc.dram_tensor("mask", [128, 128], BF16, kind="ExternalInput").ap()
    e8_d = nc.dram_tensor("e8", [32, 32], BF16, kind="ExternalInput").ap()
    sall_d = nc.dram_tensor("sall", [128, 4 * 128], BF16, kind="ExternalInput").ap()
    sbm_d = nc.dram_tensor("sbm", [12, 128], BF16, kind="ExternalInput").ap()
    ones1b_d = nc.dram_tensor("ones1b", [1, 128], BF16, kind="ExternalInput").ap()
    ones1f_d = nc.dram_tensor("ones1f", [1, 128], F32, kind="ExternalInput").ap()

    st = contextlib.ExitStack()
    with st:
        def sbt(name, shape, dt):
            return st.enter_context(nc.sbuf_tensor(name, shape, dt)).ap()

        wcomb = sbt("wcomb", [128, 8 * DIN], BF16)
        wout = sbt("wout", [128, 16 * 1024], BF16)
        wr = sbt("wr", [128, 4 * 512], BF16)
        wi = sbt("wi", [128, 4 * 512], BF16)
        nwi = sbt("nwi", [128, 4 * 512], BF16)
        wB = sbt("wB", [128, 4 * CONV_DIM], BF16)
        identb = sbt("identb_s", [128, 128], BF16)
        identf = sbt("identf_s", [128, 128], F32)
        maskc = sbt("maskc", [128, 128], BF16)
        e8c = sbt("e8c", [32, 32], BF16)
        sall = sbt("sall_s", [128, 4 * 128], BF16)
        sbm = sbt("sbm_s", [12, 128], BF16)
        ones1b = sbt("ones1b_s", [1, 128], BF16)
        ones1f = sbt("ones1f_s", [1, 128], F32)
        ones32 = sbt("ones32", [32, 128], F32)
        nw_t = sbt("nw_t", [128, 16], F32)
        convb_r = sbt("convb_r", [1, CONV_DIM], BF16)
        dtb_r = sbt("dtb_r", [1, 2 * NHEADS], BF16)
        fb_r = sbt("fb_r", [1, 1024], BF16)
        cdec_r = sbt("cdec_r", [1, NHEADS], F32)
        DBt = sbt("DBt", [128, NHEADS], F32)
        negA = sbt("negA", [32, 1], F32)
        powm = sbt("powm", [128, 1], F32)

        ubuf = sbt("ubuf", [128, 2 * D_MODEL], F32)
        xn_tm = sbt("xn_tm", [128, D_MODEL], BF16)
        xn_fm = sbt("xn_fm", [128, D_MODEL], BF16)
        t_rot = sbt("t_rot", [128, 512], BF16)
        sz2 = sbt("sz2", [128, D_INNER], BF16)
        xBC = sbt("xBC", [128, CONV_DIM], BF16)
        xw = sbt("xw", [128, CONV_DIM], BF16)
        tails = sbt("tails", [12, CONV_DIM], BF16)
        dtx = sbt("dtx", [128, 2 * 512], BF16)
        sld = sbt("sld", [128, 2 * 1024], BF16)
        xDq = sbt("xDq", [128, 512], BF16)
        carry = sbt("carry", [128, 2048], F32)
        carry_b = sbt("carry_b", [128, 2048], BF16)
        yo = sbt("yo", [128, 512], F32)
        gfm = sbt("gfm", [128, 2 * 512], BF16)
        aflat = sbt("aflat", [1, 2 * 2048], BF16)
        row_st = aflat.bitcast(F32)[:, 0:1024]
        stats = sbt("stats", [128, 12], F32)
        mv = sbt("mv", [128, 2], F32)
        rstd = sbt("rstd", [128, 1], F32)
        ssum = sbt("ssum", [128, 1], F32)
        rstd2 = sbt("rstd2", [128, 1], F32)
        tpow = sbt("tpow", [128, 1], F32)
        dt_tm = sbt("dt_tm", [128, NHEADS], F32)
        sp_rl = sbt("sp_rl", [128, NHEADS], F32)
        sp_ab = sbt("sp_ab", [128, NHEADS], F32)
        sp_v = sbt("sp_v", [128, NHEADS], F32)
        sp_t = sbt("sp_t", [128, NHEADS], F32)
        sp_s = sbt("sp_s", [128, NHEADS], F32)
        sp_w = sbt("sp_w", [128, NHEADS], F32)
        dtq_tm = sbt("dtq_tm", [128, NHEADS], BF16)
        ddq_tm = sbt("ddq_tm", [128, NHEADS], BF16)
        a_fm = sbt("a_fm", [32, 128], F32)
        dt_fm = sbt("dt_fm", [32, 128], F32)
        decay_fm = sbt("decay_fm", [32, 128], F32)
        ddq_fm = sbt("ddq_fm", [32, 128], BF16)
        a_hilo = sbt("a_hilo", [32, 256], BF16)
        nega = sbt("nega", [32, 256], BF16)
        a_tm = sbt("a_tm", [128, NHEADS], F32)
        ea_tm = sbt("ea_tm", [128, NHEADS], F32)
        cdec = sbt("cdec", [32, 1], F32)
        cdB = sbt("cdB", [128, NHEADS], F32)
        B_fm = sbt("B_fm", [128, 128], BF16)
        C_fm = sbt("C_fm", [128, 128], BF16)
        msc = sbt("msc", [128, 128], BF16)

        pbk = [st.enter_context(nc.psum_tensor(f"pb{i}", [128, 512], F32)).ap()
               for i in range(8)]

        SU = [Sem(st, nc, f"SU{i}") for i in range(2)]
        SW = [Sem(st, nc, f"SW{i}") for i in range(2)]
        SWC = Sem(st, nc, "SWC")
        SAF = [Sem(st, nc, f"SAF{i}") for i in range(2)]
        STL = Sem(st, nc, "STL")
        SO = Sem(st, nc, "SO")
        PE = Sem(st, nc, "PE")
        DV = Sem(st, nc, "DV")
        AC = Sem(st, nc, "AC")
        GP = Sem(st, nc, "GP")

        blk = st.enter_context(nc.Block())

        @blk.sync
        def _(eng):
            for dst, src in [(identb, identb_d), (identf, identf_d), (maskc, mask_d),
                             (e8c, e8_d), (sall, sall_d), (sbm, sbm_d),
                             (ones1b, ones1b_d), (ones1f, ones1f_d),
                             (cdec, alog_d), (nw_t, nw_d)]:
                SWC.dma(eng, dst, src)
            SU[0].dma(eng, ubuf[:, 0:D_MODEL], u_d[0, 0:Q, :])

        @blk.gpsimd
        def _(eng):
            eng.memset(ones32, 1.0)
            eng.memset(powm, -0.5)
            GP.bump(eng.drain())
        INIT_GP_MEMS = GP.n

        @blk.scalar
        def _(eng):
            eng.wait_ge(SWC.h, SWC.n)
            AC.bump(eng.activation(out=tpow[0:32, :], in_=cdec, func=AF.Exp,
                                   bias=0.0, scale=1.0))

        @blk.vector
        def _(eng):
            eng.wait_ge(AC.h, AC.n)
            eng.tensor_scalar(out=negA, in0=tpow[0:32, :], scalar1=-1.0, scalar2=None,
                              op0=ALU.mult)
            DV.bump(eng.drain())

        pieces = []
        for k in range(8):
            for p0 in range(0, DIN, 1024):
                w = min(1024, DIN - p0)
                pieces.append((winT_d[k * 128:(k + 1) * 128, p0:p0 + w],
                               wcomb[:, k * DIN + p0: k * DIN + p0 + w], None, w))
        for k in range(16):
            pieces.append((woutT_d[k * 128:(k + 1) * 128, :],
                           wout[:, k * 1024:(k + 1) * 1024], nw_t[:, k:k + 1], 1024))

        dv_done = {}
        for i, (src, dst, scale, w) in enumerate(pieces):
            par = i % 2
            half = carry[:, par * 1024: par * 1024 + 1024]

            @blk.sync
            def _(eng, i=i, par=par, half=half, src=src, w=w):
                if i >= 2:
                    eng.wait_ge(DV.h, dv_done[i - 2])
                SW[par].dma(eng, half[:, 0:w], src)

            @blk.vector
            def _(eng, i=i, par=par, half=half, dst=dst, scale=scale, w=w):
                eng.wait_ge(SW[par].h, SW[par].n)
                sc = 1.0 if scale is None else scale
                eng.tensor_scalar(out=dst, in0=half[:, 0:w], scalar1=sc,
                                  scalar2=None, op0=ALU.mult)
                dv_done[i] = DV.bump(eng.drain())

        row_jobs = []
        for d in range(DCONV):
            for p0 in range(0, CONV_DIM, 512):
                w = min(512, CONV_DIM - p0)
                row_jobs.append((convwT_d[d:d + 1, p0:p0 + w], w, "bcast",
                                 wB[:, d * CONV_DIM + p0: d * CONV_DIM + p0 + w]))
        for p0 in range(0, CONV_DIM, 512):
            w = min(512, CONV_DIM - p0)
            row_jobs.append((convb_d[:, p0:p0 + w], w, "row_cast", convb_r[:, p0:p0 + w]))
        for p0 in range(0, 1024, 512):
            row_jobs.append((fbias_d[:, p0:p0 + 512], 512, "row_cast", fb_r[:, p0:p0 + 512]))
        row_jobs.append((dtb_d, 32, "dtb", None))
        row_jobs.append((dpar_d, 32, "bcast", DBt))

        rj_done = {}
        for i, (src, w, kind, dst) in enumerate(row_jobs):
            par = i % 2
            half = row_st[:, par * 512: par * 512 + 512]
            bank = pbk[par]

            @blk.sync
            def _(eng, i=i, par=par, half=half, src=src, w=w):
                if i >= 2:
                    eng.wait_ge(DV.h, rj_done[i - 2])
                SW[par].dma(eng, half[:, 0:w], src)

            if kind == "bcast":
                @blk.tensor
                def _(eng, i=i, par=par, half=half, bank=bank, w=w):
                    eng.wait_ge(SW[par].h, SW[par].n)
                    eng.wait_ge(SWC.h, SWC.n)
                    if i >= 2:
                        eng.wait_ge(DV.h, rj_done[i - 2])
                    PE.bump(eng.matmul(bank[:, 0:w], ones1f, half[0:1, 0:w],
                                       start=True, stop=True))

                @blk.vector
                def _(eng, i=i, bank=bank, w=w, dst=dst):
                    eng.wait_ge(PE.h, PE.n)
                    eng.tensor_scalar(out=dst, in0=bank[:, 0:w], scalar1=1.0,
                                      scalar2=None, op0=ALU.mult)
                    rj_done[i] = DV.bump(eng.drain())
            elif kind == "row_cast":
                @blk.vector
                def _(eng, i=i, par=par, half=half, w=w, dst=dst):
                    eng.wait_ge(SW[par].h, SW[par].n)
                    eng.tensor_scalar(out=dst, in0=half[0:1, 0:w], scalar1=1.0,
                                      scalar2=None, op0=ALU.mult)
                    rj_done[i] = DV.bump(eng.drain())
            else:
                @blk.vector
                def _(eng, i=i, par=par, half=half):
                    eng.wait_ge(SW[par].h, SW[par].n)
                    eng.tensor_scalar(out=dtb_r[:, 0:32], in0=half[0:1, 0:32],
                                      scalar1=1.0, scalar2=None, op0=ALU.mult)
                    eng.drain()
                    eng.tensor_tensor(out=dtb_r[:, 32:64], in0=half[0:1, 0:32],
                                      in1=dtb_r[:, 0:32], op=ALU.subtract)
                    rj_done[i] = DV.bump(eng.drain())

        arep = carry[:, 0:128].rearrange("p (c b) -> p c b", c=4)
        brep = carry[:, 128:192].rearrange("p (c e) -> p c e", c=4)
        tmpf = carry[:, 256:512]
        accf = carry[:, 512:1024]

        ffn_jobs = []
        for (z1ap, z2ap, wdst) in [(z1_d, z2_d, wr), (z1i_d, z2i_d, wi)]:
            for mm in range(4):
                ffn_jobs.append((z1ap, z2ap, wdst, mm))

        fj_done = {}
        last_rj = max(rj_done.values()) if rj_done else 0
        for i, (z1ap, z2ap, wdst, mm) in enumerate(ffn_jobs):
            @blk.sync
            def _(eng, i=i, z1ap=z1ap, z2ap=z2ap, mm=mm):
                eng.wait_ge(DV.h, fj_done[i - 1] if i > 0 else last_rj)
                a_src = z1ap[mm * 8:(mm + 1) * 8][:, None, :, :].broadcast_to([8, 16, 4, 32])
                b_src = z2ap[None, :, :, :].broadcast_to([8, 16, 4, 16])
                SW[0].dma(eng, arep, a_src)
                SW[0].dma(eng, brep, b_src)

            @blk.vector
            def _(eng, i=i, wdst=wdst, mm=mm):
                eng.wait_ge(SW[0].h, SW[0].n)
                av = arep[:, :, :, None].broadcast_to([128, 4, 32, 16])
                bv = brep[:, :, None, :].broadcast_to([128, 4, 16, 16])
                accv = accf.rearrange("p (b e) -> p b e", b=32)
                tmpv = tmpf.rearrange("p (b e) -> p b e", b=16)
                eng.tensor_tensor(out=accv, in0=av[:, 0],
                                  in1=brep[:, 0, None, :].broadcast_to([128, 32, 16]),
                                  op=ALU.mult)
                for c in range(1, 4):
                    for hbk in range(2):
                        s0, s1 = hbk * 16, (hbk + 1) * 16
                        eng.drain()
                        eng.tensor_tensor(
                            out=tmpv, in0=av[:, c, s0:s1],
                            in1=brep[:, c, None, :].broadcast_to([128, 16, 16]),
                            op=ALU.mult)
                        eng.drain()
                        eng.tensor_tensor(out=accv[:, s0:s1], in0=accv[:, s0:s1],
                                          in1=tmpv, op=ALU.add)
                eng.drain()
                eng.tensor_scalar(out=wdst[:, mm * 512:(mm + 1) * 512], in0=accf,
                                  scalar1=1.0, scalar2=None, op0=ALU.mult)
                fj_done[i] = DV.bump(eng.drain())

        @blk.vector
        def _(eng):
            eng.tensor_scalar(out=nwi, in0=wi, scalar1=-1.0, scalar2=None, op0=ALU.mult)
            DV.bump(eng.drain())
        INIT_DV = DV.n
        INIT_GP = GP.n

        NCH = n_seq * n_chunks
        GRP = [(g * 512, min(512, DIN - g * 512)) for g in range(9)]

        fin_dv = {}
        ac_last = {}
        gfm_last = {}
        cu2_dv = {}
        cb_gp = {}
        yd_pe = {}
        st_pe = {}
        m1_pe = {}
        eexp_ac = {}
        xd_gp = {}
        sq_ac = {}
        stl_cnt = {}
        yo_pe_last = {}

        for ci in range(NCH):
            par = ci % 2
            k_in_seq = ci % n_chunks
            seq = ci // n_chunks
            first = (k_in_seq == 0)
            u_t = ubuf[:, par * D_MODEL:(par + 1) * D_MODEL]
            yd_pe[ci] = {}
            st_pe[ci] = {}
            m1_pe[ci] = {}
            eexp_ac[ci] = {}
            xd_gp[ci] = {}
            cu2_dv[ci] = {}
            cb_gp[ci] = {}
            stl_cnt[ci] = {}

            @blk.sync
            def _(eng, ci=ci):
                if ci + 1 < NCH:
                    p2 = (ci + 1) % 2
                    if ci >= 1:
                        eng.wait_ge(SO.h, 16 * ci)
                    s2, k2 = (ci + 1) // n_chunks, (ci + 1) % n_chunks
                    SU[p2].dma(eng, ubuf[:, p2 * D_MODEL:(p2 + 1) * D_MODEL],
                               u_d[s2, k2 * Q:(k2 + 1) * Q, :])

            uload_n = (ci // 2) + 1

            @blk.vector
            def _(eng, ci=ci, u_t=u_t, uload_n=uload_n):
                eng.wait_ge(SU[ci % 2].h, 16 * uload_n)
                if ci == 0:
                    eng.wait_ge(DV.h, INIT_DV)
                eng.bn_stats(stats[:, 0:6], u_t[:, 0:512])
                eng.bn_stats(stats[:, 6:12], u_t[:, 512:1024])
                eng.drain()
                eng.bn_aggr(mv, stats)
                DV.bump(eng.drain())
            mv_dv = DV.n

            @blk.gpsimd
            def _(eng, ci=ci, first=first, mv_dv=mv_dv):
                if first:
                    if ci > 0:
                        eng.wait_ge(PE.h, yo_pe_last[ci - 1])
                        eng.wait_ge(DV.h, cu2_dv[ci - 1][1])
                    else:
                        eng.wait_ge(DV.h, INIT_DV)
                    eng.memset(carry, 0.0)
                    eng.memset(carry_b, 0.0)
                    eng.drain()
                eng.wait_ge(DV.h, mv_dv)
                eng.tensor_scalar(out=tpow, in0=mv[:, 1:2], scalar1=LNEPS,
                                  scalar2=None, op0=ALU.add)
                eng.drain()
                eng.tensor_tensor(out=rstd, in0=tpow, in1=powm, op=ALU.pow)
                GP.bump(eng.drain())
            gp_rstd = GP.n

            @blk.vector
            def _(eng, u_t=u_t, gp_rstd=gp_rstd):
                eng.wait_ge(GP.h, gp_rstd)
                eng.tensor_scalar(out=xn_tm, in0=u_t, scalar1=mv[:, 0:1],
                                  scalar2=rstd, op0=ALU.subtract, op1=ALU.mult)
                DV.bump(eng.drain())
            xn_dv = DV.n

            @blk.tensor
            def _(eng, xn_dv=xn_dv, ci=ci):
                eng.wait_ge(DV.h, xn_dv)
                b5 = pbk[5].bitcast(BF16)
                for t in range(8):
                    eng.matmul(b5[:, t * 128:(t + 1) * 128], xn_tm[:, t * 128:(t + 1) * 128],
                               identb, start=True, stop=True, is_transpose=True)
                PE.bump(eng.drain())
            xnT_pe = PE.n

            @blk.vector
            def _(eng, xnT_pe=xnT_pe):
                eng.wait_ge(PE.h, xnT_pe)
                eng.tensor_copy(xn_fm, pbk[5].bitcast(BF16)[:, 0:1024])
                DV.bump(eng.drain())
            xnfm_dv = DV.n

            @blk.tensor
            def _(eng, xnfm_dv=xnfm_dv, ci=ci):
                eng.wait_ge(DV.h, xnfm_dv)
                if ci >= 1:
                    eng.wait_ge(AC.h, ac_last[ci - 1])
                for kk in range(8):
                    lhs = xn_fm[:, kk * 128:(kk + 1) * 128]
                    for g in range(5):
                        c0, w = GRP[g]
                        eng.matmul(pbk[g][:, 0:w], lhs,
                                   wcomb[:, kk * DIN + c0: kk * DIN + c0 + w],
                                   start=(kk == 0), stop=(kk == 7))
                PE.bump(eng.drain())
            pass0_pe = PE.n

            dv_prev_z = None
            for g in range(4):
                @blk.scalar
                def _(eng, g=g, p=pass0_pe, ci=ci, dv_prev_z=dv_prev_z):
                    if g == 0:
                        eng.wait_ge(PE.h, p)
                        if ci >= 1:
                            eng.wait_ge(DV.h, fin_dv[ci - 1])
                    else:
                        eng.wait_ge(DV.h, dv_prev_z)
                    eng.activation(out=t_rot, in_=pbk[g], func=AF.Tanh,
                                   bias=0.0, scale=0.5)
                    AC.bump(eng.drain())
                ac_zg = AC.n

                @blk.vector
                def _(eng, g=g, ac_zg=ac_zg):
                    eng.wait_ge(AC.h, ac_zg)
                    eng.scalar_tensor_tensor(
                        out=sz2[:, g * 512:(g + 1) * 512],
                        in0=t_rot, scalar=1.0,
                        in1=pbk[g], op0=ALU.add, op1=ALU.mult)
                    DV.bump(eng.drain())
                dv_prev_z = DV.n

            @blk.vector
            def _(eng):
                eng.tensor_copy(xBC[:, 0:512], pbk[4])
                DV.bump(eng.drain())
            evac0_dv = DV.n

            @blk.tensor
            def _(eng, evac0_dv=evac0_dv, ci=ci):
                eng.wait_ge(DV.h, evac0_dv)
                if ci >= 1:
                    eng.wait_ge(DV.h, cu2_dv[ci - 1][1])
                for kk in range(8):
                    lhs = xn_fm[:, kk * 128:(kk + 1) * 128]
                    for g in range(5, 9):
                        c0, w = GRP[g]
                        eng.matmul(pbk[g % 5][:, 0:w], lhs,
                                   wcomb[:, kk * DIN + c0: kk * DIN + c0 + w],
                                   start=(kk == 0), stop=(kk == 7 and g != 8))
                eng.matmul(pbk[3][:, 256:288], ones1b, dtb_r[:, 0:32],
                           start=False, stop=False)
                eng.matmul(pbk[3][:, 256:288], ones1b, dtb_r[:, 32:64],
                           start=False, stop=True)
                for kk in range(8):
                    lhs = xn_fm[:, kk * 128:(kk + 1) * 128]
                    rr = wr if kk < 4 else nwi
                    ii = wi if kk < 4 else wr
                    kkm = kk % 4
                    eng.matmul(pbk[5][:, 0:512], lhs, rr[:, kkm * 512:(kkm + 1) * 512],
                               start=(kk == 0), stop=False)
                    eng.matmul(pbk[6][:, 0:512], lhs, ii[:, kkm * 512:(kkm + 1) * 512],
                               start=(kk == 0), stop=False)
                eng.matmul(pbk[5][:, 0:512], ones1b, fb_r[:, 0:512], start=False, stop=True)
                eng.matmul(pbk[6][:, 0:512], ones1b, fb_r[:, 512:1024], start=False, stop=True)
                PE.bump(eng.drain())
            pass1_pe = PE.n

            @blk.vector
            def _(eng, p=pass1_pe, u_t=u_t):
                eng.wait_ge(PE.h, p)
                for g in range(5, 8):
                    c0, w = GRP[g]
                    eng.tensor_copy(xBC[:, c0 - 2048: c0 - 2048 + w], pbk[g % 5][:, 0:w])
                eng.tensor_copy(xBC[:, 2048:2304], pbk[3][:, 0:256])
                eng.drain()
                eng.tensor_tensor(out=u_t[:, 0:512], in0=u_t[:, 0:512], in1=pbk[5],
                                  op=ALU.add)
                eng.tensor_tensor(out=u_t[:, 512:1024], in0=u_t[:, 512:1024], in1=pbk[6],
                                  op=ALU.add)
                DV.bump(eng.drain())
            evac1_dv = DV.n

            @blk.scalar
            def _(eng, p=pass1_pe):
                eng.wait_ge(PE.h, p)
                dtp = pbk[3][:, 256:288]
                eng.activation(out=sp_rl, in_=dtp, func=AF.Relu, bias=0.0, scale=1.0)
                eng.activation(out=sp_ab, in_=dtp, func=AF.Abs, bias=0.0, scale=1.0)
                eng.drain()
                eng.activation(out=sp_v, in_=sp_ab, func=AF.Exp, bias=0.0, scale=1.0)
                AC.bump(eng.drain())
            ac_dt = AC.n

            @blk.vector
            def _(eng, ac_dt=ac_dt):
                eng.wait_ge(AC.h, ac_dt)
                eng.tensor_scalar(out=sp_t, in0=sp_v, scalar1=2.0, scalar2=1.0,
                                  op0=ALU.mult, op1=ALU.add)
                eng.drain()
                eng.reciprocal(sp_t, sp_t)
                eng.drain()
                eng.tensor_tensor(out=sp_s, in0=sp_t, in1=sp_t, op=ALU.mult)
                eng.drain()
                eng.tensor_scalar(out=sp_w, in0=sp_s, scalar1=1.0 / 7, scalar2=1.0 / 5,
                                  op0=ALU.mult, op1=ALU.add)
                eng.drain()
                eng.scalar_tensor_tensor(out=sp_w, in0=sp_w, scalar=0.0, in1=sp_s,
                                         op0=ALU.add, op1=ALU.mult)
                eng.drain()
                eng.tensor_scalar(out=sp_w, in0=sp_w, scalar1=1.0 / 3, scalar2=None,
                                  op0=ALU.add)
                eng.drain()
                eng.scalar_tensor_tensor(out=sp_w, in0=sp_w, scalar=0.0, in1=sp_s,
                                         op0=ALU.add, op1=ALU.mult)
                eng.drain()
                eng.tensor_scalar(out=sp_w, in0=sp_w, scalar1=1.0, scalar2=None,
                                  op0=ALU.add)
                eng.drain()
                eng.tensor_tensor(out=sp_w, in0=sp_w, in1=sp_t, op=ALU.mult)
                eng.drain()
                eng.scalar_tensor_tensor(out=dt_tm, in0=sp_w, scalar=2.0, in1=sp_rl,
                                         op0=ALU.mult, op1=ALU.add)
                eng.drain()
                eng.tensor_scalar(out=dtq_tm, in0=dt_tm, scalar1=0.25, scalar2=None,
                                  op0=ALU.mult)
                DV.bump(eng.drain())
            dt_dv = DV.n

            @blk.tensor
            def _(eng, dt_dv=dt_dv):
                eng.wait_ge(DV.h, dt_dv)
                eng.matmul(pbk[7][0:32, 0:128], dt_tm[:, 0:32], identf,
                           start=True, stop=True, is_transpose=True)
                PE.bump(eng.drain())
            dtT_pe = PE.n

            @blk.vector
            def _(eng, dtT_pe=dtT_pe):
                eng.wait_ge(PE.h, dtT_pe)
                dtfmp = pbk[7][0:32, 0:128]
                eng.tensor_copy(dt_fm, dtfmp)
                eng.tensor_scalar(out=decay_fm, in0=dtfmp, scalar1=negA, scalar2=None,
                                  op0=ALU.mult)
                eng.drain()
                eng.tensor_tensor_scan(out=a_fm, data0=ones32, data1=decay_fm,
                                       initial=0.0, op0=ALU.mult, op1=ALU.add)
                eng.drain()
                eng.tensor_scalar(out=a_hilo[:, 0:128], in0=a_fm, scalar1=1.0,
                                  scalar2=None, op0=ALU.mult)
                eng.drain()
                eng.tensor_tensor(out=a_hilo[:, 128:256], in0=a_fm, in1=a_hilo[:, 0:128],
                                  op=ALU.subtract)
                eng.drain()
                eng.tensor_scalar(out=nega, in0=a_hilo, scalar1=-1.0, scalar2=None,
                                  op0=ALU.mult)
                DV.bump(eng.drain())
            a_dv = DV.n

            @blk.tensor
            def _(eng, a_dv=a_dv):
                eng.wait_ge(DV.h, a_dv)
                eng.matmul(pbk[7][:, 128:160], a_fm, identf[0:32, 0:32],
                           start=True, stop=True, is_transpose=True)
                PE.bump(eng.drain())
            aT_pe = PE.n

            @blk.vector
            def _(eng, aT_pe=aT_pe):
                eng.wait_ge(PE.h, aT_pe)
                eng.tensor_copy(a_tm, pbk[7][:, 128:160])
                DV.bump(eng.drain())
            atm_dv = DV.n

            @blk.scalar
            def _(eng, atm_dv=atm_dv):
                eng.wait_ge(DV.h, atm_dv)
                eng.activation(out=ea_tm, in_=a_tm, func=AF.Exp, bias=0.0, scale=1.0)
                eng.activation(out=decay_fm, in_=a_fm, func=AF.Exp,
                               bias=a_fm[:, 127:128], scale=-1.0)
                eng.activation(out=cdec, in_=a_fm[:, 127:128], func=AF.Exp,
                               bias=0.0, scale=1.0)
                AC.bump(eng.drain())
            ac_ea = AC.n

            @blk.vector
            def _(eng, ac_ea=ac_ea):
                eng.wait_ge(AC.h, ac_ea)
                eng.scalar_tensor_tensor(out=ddq_fm, in0=decay_fm, scalar=0.25,
                                         in1=dt_fm, op0=ALU.mult, op1=ALU.mult)
                DV.bump(eng.drain())
            ddqfm_dv = DV.n

            @blk.tensor
            def _(eng, ddqfm_dv=ddqfm_dv):
                eng.wait_ge(DV.h, ddqfm_dv)
                b7h = pbk[7].bitcast(BF16)
                eng.matmul(b7h[0:128, 512:544], ddq_fm, identb[0:32, 0:32],
                           start=True, stop=True, is_transpose=True)
                eng.matmul(pbk[7][0:1, 160:192], cdec, identf[0:32, 0:32],
                           start=True, stop=True, is_transpose=True)
                PE.bump(eng.drain())
            ddqT_pe = PE.n

            @blk.vector
            def _(eng, ddqT_pe=ddqT_pe):
                eng.wait_ge(PE.h, ddqT_pe)
                eng.tensor_copy(ddq_tm, pbk[7].bitcast(BF16)[:, 512:544])
                eng.tensor_copy(cdec_r, pbk[7][0:1, 160:192])
                DV.bump(eng.drain())
            cdecr_dv = DV.n

            @blk.tensor
            def _(eng, cdecr_dv=cdecr_dv):
                eng.wait_ge(DV.h, cdecr_dv)
                eng.matmul(pbk[7][:, 192:224], ones1f, cdec_r[:, 0:32],
                           start=True, stop=True)
                PE.bump(eng.drain())
            cdB_pe = PE.n

            @blk.vector
            def _(eng, cdB_pe=cdB_pe, ci=ci):
                eng.wait_ge(PE.h, cdB_pe)
                if ci >= 1:
                    eng.wait_ge(GP.h, cb_gp[ci - 1][1])
                eng.tensor_copy(cdB, pbk[7][:, 192:224])
                DV.bump(eng.drain())
            cdB_dv = DV.n

            @blk.tensor
            def _(eng, ci=ci, first=first, evac1_dv=evac1_dv, ac_dt=ac_dt):
                eng.wait_ge(DV.h, evac1_dv)
                eng.wait_ge(AC.h, ac_dt)
                if not first:
                    eng.wait_ge(STL.h, 16 * STL_total_prev[ci])
                    for g in range(5):
                        c0 = g * 512
                        w = min(512, CONV_DIM - c0)
                        eng.matmul(pbk[g][:, 0:w], sbm, tails[:, c0:c0 + w],
                                   start=True, stop=False)
                PE.bump(eng.drain())
            if ci == 0:
                STL_total_prev = {}
            bnd_pe = PE.n

            xw_dv = {}
            cnv_pe = {}
            for d in range(DCONV):
                @blk.vector
                def _(eng, d=d, ci=ci, evac1_dv=evac1_dv):
                    eng.wait_ge(DV.h, evac1_dv)
                    if d == 0:
                        if ci >= 1:
                            eng.wait_ge(AC.h, sq_ac[ci - 1])
                    else:
                        eng.wait_ge(PE.h, cnv_pe[d - 1])
                        eng.wait_ge(STL.h, 16 * stl_cnt[ci][d - 1])
                    eng.tensor_tensor(out=xw, in0=xBC,
                                      in1=wB[:, d * CONV_DIM:(d + 1) * CONV_DIM],
                                      op=ALU.mult)
                    DV.bump(eng.drain())
                xw_dv[d] = DV.n

                @blk.tensor
                def _(eng, d=d, xwdv=xw_dv[d], first=first):
                    eng.wait_ge(DV.h, xwdv)
                    lhs = sall[:, d * 128:(d + 1) * 128]
                    for g in range(5):
                        c0 = g * 512
                        w = min(512, CONV_DIM - c0)
                        eng.matmul(pbk[g][:, 0:w], lhs, xw[:, c0:c0 + w],
                                   start=(d == 0 and first), stop=False)
                    PE.bump(eng.drain())
                cnv_pe[d] = PE.n

                @blk.sync
                def _(eng, d=d, ci=ci, p=cnv_pe[d]):
                    eng.wait_ge(PE.h, p)
                    STL.dma(eng, tails[d * 3:(d + 1) * 3, :], xw[125:128, :])
                stl_cnt[ci][d] = STL.n // 16
            STL_total_prev[ci + 1] = STL.n // 16

            @blk.tensor
            def _(eng):
                for g in range(5):
                    c0 = g * 512
                    w = min(512, CONV_DIM - c0)
                    eng.matmul(pbk[g][:, 0:w], ones1b, convb_r[:, c0:c0 + w],
                               start=False, stop=True)
                PE.bump(eng.drain())
            conv_pe = PE.n

            dv_prev_c = None
            dv_cg = {}
            for g in range(5):
                w = min(512, CONV_DIM - g * 512)
                @blk.scalar
                def _(eng, g=g, w=w, conv_pe=conv_pe, dv_prev_c=dv_prev_c):
                    if g == 0:
                        eng.wait_ge(PE.h, conv_pe)
                    else:
                        eng.wait_ge(DV.h, dv_prev_c)
                    eng.activation(out=t_rot[:, 0:w], in_=pbk[g][:, 0:w],
                                   func=AF.Tanh, bias=0.0, scale=0.5)
                    AC.bump(eng.drain())
                ac_cg = AC.n

                @blk.vector
                def _(eng, g=g, w=w, ac_cg=ac_cg):
                    eng.wait_ge(AC.h, ac_cg)
                    eng.scalar_tensor_tensor(
                        out=xBC[:, g * 512:g * 512 + w],
                        in0=t_rot[:, 0:w], scalar=1.0,
                        in1=pbk[g][:, 0:w], op0=ALU.add, op1=ALU.mult)
                    DV.bump(eng.drain())
                dv_prev_c = DV.n
                dv_cg[g] = DV.n
            dv_c01 = dv_cg[1]
            dv_c23 = dv_cg[3]
            ac_c01 = None
            ac_c23 = None
            xbca_dv = DV.n

            @blk.tensor
            def _(eng, xbca_dv=xbca_dv):
                eng.wait_ge(DV.h, xbca_dv)
                b7h = pbk[7].bitcast(BF16)
                eng.matmul(b7h[:, 0:128], xBC[:, 2048:2176], identb,
                           start=True, stop=True, is_transpose=True)
                eng.matmul(b7h[:, 128:256], xBC[:, 2176:2304], identb,
                           start=True, stop=True, is_transpose=True)
                PE.bump(eng.drain())
            bcT_pe = PE.n

            @blk.vector
            def _(eng, bcT_pe=bcT_pe):
                eng.wait_ge(PE.h, bcT_pe)
                b7h = pbk[7].bitcast(BF16)
                eng.tensor_copy(B_fm, b7h[:, 0:128])
                eng.tensor_copy(C_fm, b7h[:, 128:256])
                DV.bump(eng.drain())
            bc_dv = DV.n

            @blk.tensor
            def _(eng, bc_dv=bc_dv):
                eng.wait_ge(DV.h, bc_dv)
                eng.matmul(pbk[6][:, 0:128], B_fm, C_fm, start=True, stop=True)
                PE.bump(eng.drain())
            sc_pe = PE.n

            @blk.vector
            def _(eng, sc_pe=sc_pe):
                eng.wait_ge(PE.h, sc_pe)
                eng.tensor_copy(msc, pbk[6][:, 0:128])
                DV.bump(eng.drain())
            msc_dv = DV.n

            yo_done_dv = {}
            yoMM_pe = {}
            for half in range(2):
                @blk.tensor
                def _(eng, half=half, ci=ci, first=first, msc_dv=msc_dv,
                      gp_rstd=gp_rstd):
                    if half == 0:
                        eng.wait_ge(DV.h, msc_dv)
                    else:
                        eng.wait_ge(DV.h, cu2_dv[ci][0])
                    if first:
                        eng.wait_ge(GP.h, gp_rstd)
                    else:
                        eng.wait_ge(GP.h, cb_gp[ci - 1][half])
                    for qq in range(2):
                        eng.matmul(pbk[6 + qq][:, 0:512], C_fm,
                                   carry_b[:, half * 1024 + qq * 512:
                                           half * 1024 + (qq + 1) * 512],
                                   start=True, stop=True)
                    PE.bump(eng.drain())
                yoMM_pe[half] = PE.n

                for qq in range(2):
                    q = half * 2 + qq
                    qp = q % 2
                    pb0 = pbk[0 + 2 * qp]
                    pb1 = pbk[1 + 2 * qp]
                    sldb = sld[:, qp * 1024:(qp + 1) * 1024]
                    dtxb = dtx[:, qp * 512:(qp + 1) * 512]

                    @blk.sync
                    def _(eng, q=q, qp=qp, ci=ci, a_dv=a_dv):
                        eng.wait_ge(DV.h, a_dv)
                        if q >= 2:
                            eng.wait_ge(PE.h, m1_pe[ci][q - 2])
                        elif ci >= 1:
                            eng.wait_ge(PE.h, m1_pe[ci - 1][q + 2])
                        dst = aflat[:, qp * 2048:(qp + 1) * 2048].rearrange(
                            "p (r c) -> p r c", r=8)
                        SAF[qp].dma(eng, dst, a_hilo[8 * q:8 * (q + 1), :])
                    saf_n = SAF[qp].n

                    @blk.tensor
                    def _(eng, q=q, qp=qp, ci=ci, pb0=pb0, pb1=pb1, saf_n=saf_n,
                          dv_c01=dv_c01, dv_c23=dv_c23):
                        eng.wait_ge(SAF[qp].h, 16 * (saf_n // 16))
                        if q >= 2:
                            eng.wait_ge(AC.h, eexp_ac[ci][q - 2])
                        else:
                            eng.wait_ge(DV.h, dv_c01 if qp == 0 else dv_c23)
                        flv = aflat[:, qp * 2048:(qp + 1) * 2048].rearrange(
                            "p (r c) -> p r c", r=8)
                        for ih, pbx in ((0, pb0), (1, pb1)):
                            eng.matmul(pbx.rearrange("p (h i) -> p h i", h=4),
                                       ones1b, flv[:, 4 * ih:4 * (ih + 1), 0:128],
                                       start=True, stop=False)
                            eng.matmul(pbx.rearrange("p (h i) -> p h i", h=4),
                                       ones1b, flv[:, 4 * ih:4 * (ih + 1), 128:256],
                                       start=False, stop=False)
                            e8v = e8c[:, q * 8:(q + 1) * 8][:, :, None].broadcast_to(
                                [32, 8, 128])
                            eng.matmul(pbx.rearrange("p (h i) -> p h i", h=4),
                                       nega[:, 0:128], e8v[:, 4 * ih:4 * (ih + 1), :],
                                       start=False, stop=False)
                            eng.matmul(pbx.rearrange("p (h i) -> p h i", h=4),
                                       nega[:, 128:256], e8v[:, 4 * ih:4 * (ih + 1), :],
                                       start=False, stop=False)
                            mvv = maskc[:, None, :].broadcast_to([128, 4, 128])
                            eng.matmul(pbx.rearrange("p (h i) -> p h i", h=4),
                                       identb, mvv, start=False, stop=True)
                        m1_pe[ci][q] = PE.bump(eng.drain())

                    @blk.scalar
                    def _(eng, q=q, ci=ci, pb0=pb0, pb1=pb1, sldb=sldb):
                        eng.wait_ge(PE.h, m1_pe[ci][q])
                        if q >= 2:
                            eng.wait_ge(PE.h, yd_pe[ci][q - 2])
                        elif ci >= 1:
                            eng.wait_ge(PE.h, yd_pe[ci - 1][q + 2])
                        eng.activation(out=sldb[:, 0:512], in_=pb0, func=AF.Exp,
                                       bias=0.0, scale=1.0)
                        eng.activation(out=sldb[:, 512:1024], in_=pb1, func=AF.Exp,
                                       bias=0.0, scale=1.0)
                        eexp_ac[ci][q] = AC.bump(eng.drain())

                    @blk.gpsimd
                    def _(eng, q=q, ci=ci, xbca_dv=xbca_dv):
                        eng.wait_ge(DV.h, xbca_dv)
                        if q >= 1:
                            eng.wait_ge(PE.h, yd_pe[ci][q - 1])
                        elif ci >= 1:
                            eng.wait_ge(PE.h, yd_pe[ci - 1][3])
                        xv = xBC[:, 0:2048].rearrange("p (h x) -> p h x", h=32)[
                            :, 8 * q:8 * (q + 1), :]
                        dv_ = DBt[:, 8 * q:8 * (q + 1), None].broadcast_to([128, 8, 64])
                        eng.tensor_tensor(out=xDq.rearrange("p (h x) -> p h x", h=8),
                                          in0=xv, in1=dv_, op=ALU.mult)
                        xd_gp[ci][q] = GP.bump(eng.drain())

                    @blk.vector
                    def _(eng, q=q, qp=qp, ci=ci, sldb=sldb, dtxb=dtxb, dt_dv=dt_dv,
                          msc_dv=msc_dv):
                        eng.wait_ge(AC.h, eexp_ac[ci][q])
                        sv = sldb.rearrange("p (h i) -> p h i", h=8)
                        eng.tensor_tensor(out=sv, in0=sv,
                                          in1=msc[:, None, :].broadcast_to([128, 8, 128]),
                                          op=ALU.mult)
                        if q >= 2:
                            eng.wait_ge(PE.h, st_pe[ci][q - 2])
                        elif ci >= 1:
                            eng.wait_ge(PE.h, st_pe[ci - 1][q + 2])
                        eng.drain()
                        xv = xBC[:, 0:2048].rearrange("p (h x) -> p h x", h=32)[
                            :, 8 * q:8 * (q + 1), :]
                        eng.tensor_tensor(out=dtxb.rearrange("p (h x) -> p h x", h=8),
                                          in0=xv,
                                          in1=dtq_tm[:, 8 * q:8 * (q + 1), None
                                                     ].broadcast_to([128, 8, 64]),
                                          op=ALU.mult)
                        DV.bump(eng.drain())
                    sld_dv = DV.n

                    @blk.tensor
                    def _(eng, q=q, qp=qp, ci=ci, sld_dv=sld_dv, sldb=sldb, dtxb=dtxb,
                          half=half):
                        eng.wait_ge(DV.h, sld_dv)
                        eng.wait_ge(GP.h, xd_gp[ci][q])
                        ydb = pbk[4 + qp]
                        for hh in range(8):
                            eng.matmul(ydb[:, hh * 64:(hh + 1) * 64],
                                       sldb[:, hh * 128:(hh + 1) * 128],
                                       dtxb[:, hh * 64:(hh + 1) * 64],
                                       start=(hh == 0), stop=False)
                        eng.matmul(ydb[:, 0:512], identb, xDq, start=False, stop=True)
                        yd_pe[ci][q] = PE.bump(eng.drain())

                for qq in range(2):
                    q = half * 2 + qq
                    @blk.vector
                    def _(eng, q=q, qq=qq, ci=ci, half=half):
                        eng.wait_ge(PE.h, yd_pe[ci][q])
                        eng.wait_ge(PE.h, yoMM_pe[half])
                        yob = pbk[6 + qq]
                        eav = ea_tm[:, 8 * q:8 * (q + 1), None].broadcast_to([128, 8, 64])
                        eng.tensor_tensor(out=yo.rearrange("p (h x) -> p h x", h=8),
                                          in0=yob.rearrange("p (h x) -> p h x", h=8),
                                          in1=eav, op=ALU.mult)
                        eng.drain()
                        eng.tensor_tensor(out=yo, in0=yo, in1=pbk[4 + (q % 2)],
                                          op=ALU.add)
                        eng.drain()
                        gq = sz2[:, q * 512:(q + 1) * 512]
                        eng.tensor_tensor(out=gq, in0=yo, in1=gq, op=ALU.mult)
                        yo_done_dv[q] = DV.bump(eng.drain())

                for qq in range(2):
                    q = half * 2 + qq
                    qp = q % 2

                    @blk.vector
                    def _(eng, q=q, qp=qp, ci=ci):
                        eng.wait_ge(PE.h, yd_pe[ci][q])
                        dtxb = dtx[:, qp * 512:(qp + 1) * 512]
                        xv = xBC[:, 0:2048].rearrange("p (h x) -> p h x", h=32)[
                            :, 8 * q:8 * (q + 1), :]
                        eng.tensor_tensor(out=dtxb.rearrange("p (h x) -> p h x", h=8),
                                          in0=xv,
                                          in1=ddq_tm[:, 8 * q:8 * (q + 1), None
                                                     ].broadcast_to([128, 8, 64]),
                                          op=ALU.mult)
                        DV.bump(eng.drain())
                    ddtx_dv = DV.n

                    @blk.tensor
                    def _(eng, q=q, qq=qq, qp=qp, ci=ci, ddtx_dv=ddtx_dv):
                        eng.wait_ge(DV.h, ddtx_dv)
                        eng.wait_ge(DV.h, yo_done_dv[q])
                        eng.matmul(pbk[6 + qq][:, 0:512], xBC[:, 2048:2176],
                                   dtx[:, qp * 512:(qp + 1) * 512],
                                   start=True, stop=True)
                        st_pe[ci][q] = PE.bump(eng.drain())

                @blk.gpsimd
                def _(eng, half=half, ci=ci, cdB_dv=cdB_dv, first=first):
                    eng.wait_ge(DV.h, cdB_dv)
                    if ci >= 1 and not first:
                        eng.wait_ge(DV.h, cu2_dv[ci - 1][half])
                    ch = carry[:, half * 1024:(half + 1) * 1024]
                    cv = cdB[:, 16 * half:16 * (half + 1), None].broadcast_to(
                        [128, 16, 64])
                    eng.tensor_tensor(out=ch.rearrange("p (h x) -> p h x", h=16),
                                      in0=ch.rearrange("p (h x) -> p h x", h=16),
                                      in1=cv, op=ALU.mult)
                    GP.bump(eng.drain())
                cu1_gp = GP.n

                @blk.vector
                def _(eng, half=half, ci=ci, cu1_gp=cu1_gp):
                    eng.wait_ge(GP.h, cu1_gp)
                    eng.wait_ge(PE.h, st_pe[ci][half * 2 + 1])
                    ch = carry[:, half * 1024:(half + 1) * 1024]
                    eng.tensor_tensor(out=ch[:, 0:512], in0=ch[:, 0:512], in1=pbk[6],
                                      op=ALU.add)
                    eng.drain()
                    eng.tensor_tensor(out=ch[:, 512:1024], in0=ch[:, 512:1024],
                                      in1=pbk[7], op=ALU.add)
                    cu2_dv[ci][half] = DV.bump(eng.drain())

                @blk.gpsimd
                def _(eng, half=half, ci=ci):
                    eng.wait_ge(DV.h, cu2_dv[ci][half])
                    ch = carry[:, half * 1024:(half + 1) * 1024]
                    eng.tensor_scalar(out=carry_b[:, half * 1024:(half + 1) * 1024],
                                      in0=ch, scalar1=1.0, scalar2=None, op0=ALU.mult)
                    cb_gp[ci][half] = GP.bump(eng.drain())

            g_all_dv = max(yo_done_dv.values())
            yo_pe_last[ci] = yoMM_pe[1]

            @blk.scalar
            def _(eng, ci=ci, g_all_dv=g_all_dv):
                eng.wait_ge(DV.h, g_all_dv)
                eng.wait_ge(STL.h, 16 * STL_total_prev[ci + 1])
                eng.activation(out=xw[:, 0:2048], in_=sz2, func=AF.Square,
                               bias=0.0, scale=1.0, accum_out=ssum)
                sq_ac[ci] = AC.bump(eng.drain())
            ac_last[ci] = AC.n

            @blk.gpsimd
            def _(eng, ci=ci):
                eng.wait_ge(AC.h, sq_ac[ci])
                eng.tensor_scalar(out=tpow, in0=ssum, scalar1=1.0 / D_INNER,
                                  scalar2=RMSEPS, op0=ALU.mult, op1=ALU.add)
                eng.drain()
                eng.tensor_tensor(out=rstd2, in0=tpow, in1=powm, op=ALU.pow)
                GP.bump(eng.drain())
            rms_gp = GP.n

            for bt in range(4):
                @blk.tensor
                def _(eng, bt=bt, ci=ci, g_all_dv=g_all_dv):
                    if bt == 0:
                        eng.wait_ge(DV.h, g_all_dv)
                        eng.wait_ge(AC.h, eexp_ac[ci][3])
                    b01 = pbk[bt % 2].bitcast(BF16)
                    for t in range(4):
                        tt_ = bt * 4 + t
                        eng.matmul(b01[:, t * 128:(t + 1) * 128],
                                   sz2[:, tt_ * 128:(tt_ + 1) * 128],
                                   identb, start=True, stop=True, is_transpose=True)
                    PE.bump(eng.drain())
                gT_pe = PE.n

                @blk.vector
                def _(eng, bt=bt, gT_pe=gT_pe):
                    eng.wait_ge(PE.h, gT_pe)
                    eng.tensor_copy(gfm[:, (bt % 2) * 512:(bt % 2 + 1) * 512],
                                    pbk[bt % 2].bitcast(BF16)[:, 0:512])
                    DV.bump(eng.drain())
                gfm_dv = DV.n

                @blk.tensor
                def _(eng, bt=bt, ci=ci, gfm_dv=gfm_dv):
                    eng.wait_ge(DV.h, gfm_dv)
                    for t in range(4):
                        kk = bt * 4 + t
                        lhs = gfm[:, (bt % 2) * 512 + t * 128:(bt % 2) * 512 + (t + 1) * 128]
                        for e in range(2):
                            eng.matmul(pbk[4 + e][:, 0:512], lhs,
                                       wout[:, kk * 1024 + e * 512:
                                            kk * 1024 + e * 512 + 512],
                                       start=(bt == 0 and t == 0),
                                       stop=(bt == 3 and t == 3))
                    PE.bump(eng.drain())
            op_pe = PE.n
            gfm_last[ci] = DV.n

            @blk.vector
            def _(eng, op_pe=op_pe, rms_gp=rms_gp, u_t=u_t):
                eng.wait_ge(PE.h, op_pe)
                eng.wait_ge(GP.h, rms_gp)
                for e in range(2):
                    eng.scalar_tensor_tensor(
                        out=u_t[:, e * 512:(e + 1) * 512],
                        in0=pbk[4 + e], scalar=rstd2,
                        in1=u_t[:, e * 512:(e + 1) * 512],
                        op0=ALU.mult, op1=ALU.add)
                fin_dv[ci] = DV.bump(eng.drain())

            @blk.sync
            def _(eng, ci=ci, seq=seq, k=k_in_seq, u_t=u_t):
                eng.wait_ge(DV.h, fin_dv[ci])
                SO.dma(eng, out_d[seq, k * Q:(k + 1) * Q, :], u_t)

        @blk.sync
        def _(eng):
            eng.wait_ge(SO.h, SO.n)

    return nc


def _host_prep(inputs):
    W_in = np.asarray(inputs["L0_attn_W_in"])
    W_out = np.asarray(inputs["L0_attn_W_out"])
    prep = {}
    prep["winT"] = np.ascontiguousarray(W_in.T)
    prep["woutT"] = np.ascontiguousarray(W_out.T)
    prep["nw"] = np.ascontiguousarray(
        np.asarray(inputs["L0_attn_norm_w"]).reshape(16, 128).T)
    prep["convwT"] = np.ascontiguousarray(np.asarray(inputs["L0_attn_conv_w"]).T)
    prep["convb"] = np.ascontiguousarray(np.asarray(inputs["L0_attn_conv_b"]).reshape(1, -1))
    prep["dtb"] = np.ascontiguousarray(np.asarray(inputs["L0_attn_dt_bias"]).reshape(1, -1))
    prep["alog"] = np.ascontiguousarray(np.asarray(inputs["L0_attn_A_log"]).reshape(-1, 1))
    prep["dpar"] = np.ascontiguousarray(np.asarray(inputs["L0_attn_D_param"]).reshape(1, -1))
    prep["fbias"] = np.ascontiguousarray(np.concatenate(
        [np.asarray(inputs["L0_ffn_bias_real"]),
         np.asarray(inputs["L0_ffn_bias_imag"])]).reshape(1, -1))
    prep["z1"] = np.ascontiguousarray(
        np.transpose(np.asarray(inputs["L0_ffn_G1_R"])[0], (2, 1, 0)))
    prep["z2"] = np.ascontiguousarray(
        np.transpose(np.asarray(inputs["L0_ffn_G2_R"])[:, :, 0, :], (2, 0, 1)))
    prep["z1i"] = np.ascontiguousarray(
        np.transpose(np.asarray(inputs["L0_ffn_G1_I"])[0], (2, 1, 0)))
    prep["z2i"] = np.ascontiguousarray(
        np.transpose(np.asarray(inputs["L0_ffn_G2_I"])[:, :, 0, :], (2, 0, 1)))
    prep["identb"] = np.eye(128, dtype=BF)
    prep["identf"] = np.eye(128, dtype=np.float32)
    mask = np.zeros((128, 128), np.float32)
    for j in range(128):
        mask[j, :j] = NEG
    prep["mask"] = mask.astype(BF)
    e8 = np.zeros((32, 32), np.float32)
    for kq in range(32):
        e8[kq, kq] = 1.0
    prep["e8"] = e8.astype(BF)
    sall = np.zeros((128, 4 * 128), np.float32)
    for d in range(4):
        for t in range(128):
            j = t - 3 + d
            if j >= 0:
                sall[j, d * 128 + t] = 1.0
    prep["sall"] = sall.astype(BF)
    sbm = np.zeros((12, 128), np.float32)
    for d in range(4):
        for r in range(3):
            t = r - d
            if t >= 0:
                sbm[d * 3 + r, t] = 1.0
    prep["sbm"] = sbm.astype(BF)
    prep["ones1b"] = np.ones((1, 128), BF)
    prep["ones1f"] = np.ones((1, 128), np.float32)
    return prep


def kernel(**inputs):
    u = np.asarray(inputs["L0_attn_u"])
    prep = _host_prep(inputs)
    nc = build_nc(2, 32)
    in_maps = []
    for c in range(8):
        m = dict(prep)
        m["u"] = np.ascontiguousarray(u[2 * c:2 * c + 2])
        in_maps.append(m)
    res = run_bass_kernel_spmd(nc, in_maps, core_ids=list(range(8)))
    return np.concatenate([r["out"] for r in res.results], axis=0)


# version 02
# speedup vs baseline: 1.0550x; 1.0550x over previous
import numpy as np
import ml_dtypes
import contextlib
import concourse.bass as bass
import concourse.mybir as mybir
from concourse.bass_utils import run_bass_kernel_spmd

F32 = mybir.dt.float32
BF16 = mybir.dt.bfloat16
AF = mybir.ActivationFunctionType
ALU = mybir.AluOpType
BF = ml_dtypes.bfloat16

D_MODEL = 1024
D_INNER = 2048
CONV_DIM = 2304
NHEADS = 32
DCONV = 4
DIN = 4384
Q = 128
NEG = -1e30
LNEPS = 1e-5
RMSEPS = 16e-5


class R:
    def __init__(self, eng):
        self._e = eng
        self.last = None

    def __getattr__(self, k):
        f = getattr(self._e, k)
        if callable(f):
            def w(*a, **kw):
                r = f(*a, **kw)
                if r is not None:
                    self.last = r
                return r
            return w
        return f


class Sem:
    def __init__(self, st, nc, name):
        self.h = st.enter_context(nc.semaphore(name))
        self.n = 0

    def bump(self, instr, k=1):
        self.n += k
        instr.then_inc(self.h, k)
        return self.n

    def dma(self, eng, out, in_):
        eng.wait_ge(self.h, self.n)
        return self.bump(eng.dma_start(out=out, in_=in_), 16)


def build_nc(n_seq, n_chunks, races=False):
    nc = bass.Bass("TRN2", target_bir_lowering=False, detect_race_conditions=races)

    u_d = nc.dram_tensor("u", [n_seq, n_chunks * Q, D_MODEL], F32, kind="ExternalInput").ap()
    out_d = nc.dram_tensor("out", [n_seq, n_chunks * Q, D_MODEL], F32, kind="ExternalOutput").ap()
    winT_d = nc.dram_tensor("winT", [D_MODEL, DIN], F32, kind="ExternalInput").ap()
    woutT_d = nc.dram_tensor("woutT", [D_INNER, D_MODEL], F32, kind="ExternalInput").ap()
    nw_d = nc.dram_tensor("nw", [128, 16], F32, kind="ExternalInput").ap()
    convwT_d = nc.dram_tensor("convwT", [DCONV, CONV_DIM], F32, kind="ExternalInput").ap()
    convb_d = nc.dram_tensor("convb", [1, CONV_DIM], F32, kind="ExternalInput").ap()
    dtb_d = nc.dram_tensor("dtb", [1, NHEADS], F32, kind="ExternalInput").ap()
    alog_d = nc.dram_tensor("alog", [NHEADS, 1], F32, kind="ExternalInput").ap()
    dpar_d = nc.dram_tensor("dpar", [1, NHEADS], F32, kind="ExternalInput").ap()
    fbias_d = nc.dram_tensor("fbias", [1, 1024], F32, kind="ExternalInput").ap()
    z1_d = nc.dram_tensor("z1", [32, 4, 32], F32, kind="ExternalInput").ap()
    z2_d = nc.dram_tensor("z2", [16, 4, 16], F32, kind="ExternalInput").ap()
    z1i_d = nc.dram_tensor("z1i", [32, 4, 32], F32, kind="ExternalInput").ap()
    z2i_d = nc.dram_tensor("z2i", [16, 4, 16], F32, kind="ExternalInput").ap()
    identb_d = nc.dram_tensor("identb", [128, 128], BF16, kind="ExternalInput").ap()
    identf_d = nc.dram_tensor("identf", [128, 128], F32, kind="ExternalInput").ap()
    mask_d = nc.dram_tensor("mask", [128, 128], BF16, kind="ExternalInput").ap()
    e8_d = nc.dram_tensor("e8", [32, 32], BF16, kind="ExternalInput").ap()
    sall_d = nc.dram_tensor("sall", [128, 4 * 128], BF16, kind="ExternalInput").ap()
    sbm_d = nc.dram_tensor("sbm", [12, 128], BF16, kind="ExternalInput").ap()
    ones1b_d = nc.dram_tensor("ones1b", [1, 128], BF16, kind="ExternalInput").ap()
    ones1f_d = nc.dram_tensor("ones1f", [1, 128], F32, kind="ExternalInput").ap()

    st = contextlib.ExitStack()
    with st:
        def sbt(name, shape, dt):
            return st.enter_context(nc.sbuf_tensor(name, shape, dt)).ap()

        wcomb = sbt("wcomb", [128, 8 * DIN], BF16)
        wout = sbt("wout", [128, 16 * 1024], BF16)
        wr = sbt("wr", [128, 4 * 512], BF16)
        wi = sbt("wi", [128, 4 * 512], BF16)
        nwi = sbt("nwi", [128, 4 * 512], BF16)
        wB = sbt("wB", [128, 4 * CONV_DIM], BF16)
        identb = sbt("identb_s", [128, 128], BF16)
        identf = sbt("identf_s", [128, 128], F32)
        maskc = sbt("maskc", [128, 128], BF16)
        e8c = sbt("e8c", [32, 32], BF16)
        sall = sbt("sall_s", [128, 4 * 128], BF16)
        sbm = sbt("sbm_s", [12, 128], BF16)
        ones1b = sbt("ones1b_s", [1, 128], BF16)
        ones1f = sbt("ones1f_s", [1, 128], F32)
        ones32 = sbt("ones32", [32, 128], F32)
        nw_t = sbt("nw_t", [128, 16], F32)
        convb_r = sbt("convb_r", [1, CONV_DIM], BF16)
        dtb_r = sbt("dtb_r", [1, 2 * NHEADS], BF16)
        fb_r = sbt("fb_r", [1, 1024], BF16)
        cdec_r = sbt("cdec_r", [1, NHEADS], F32)
        DBt = sbt("DBt", [128, NHEADS], F32)
        negA = sbt("negA", [32, 1], F32)
        powm = sbt("powm", [128, 1], F32)

        ubuf = sbt("ubuf", [128, 2 * D_MODEL], F32)
        xn_tm = sbt("xn_tm", [128, D_MODEL], BF16)
        xn_fm = sbt("xn_fm", [128, D_MODEL], BF16)
        t_rot = sbt("t_rot", [128, 512], BF16)
        sz2 = sbt("sz2", [128, D_INNER], BF16)
        xBC = sbt("xBC", [128, CONV_DIM], BF16)
        xw = sbt("xw", [128, CONV_DIM], BF16)
        tails = sbt("tails", [12, CONV_DIM], BF16)
        dtx = sbt("dtx", [128, 2 * 512], BF16)
        sld = sbt("sld", [128, 2 * 1024], BF16)
        xDq = sbt("xDq", [128, 512], BF16)
        carry = sbt("carry", [128, 2048], F32)
        carry_b = sbt("carry_b", [128, 2048], BF16)
        yo = sbt("yo", [128, 512], F32)
        gfm = sbt("gfm", [128, 2 * 512], BF16)
        aflat = sbt("aflat", [1, 2 * 2048], BF16)
        row_st = aflat.bitcast(F32)[:, 0:1024]
        stats = sbt("stats", [128, 12], F32)
        mv = sbt("mv", [128, 2], F32)
        rstd = sbt("rstd", [128, 1], F32)
        ssum = sbt("ssum", [128, 1], F32)
        rstd2 = sbt("rstd2", [128, 1], F32)
        tpow = sbt("tpow", [128, 1], F32)
        dt_tm = sbt("dt_tm", [128, NHEADS], F32)
        sp_rl = sbt("sp_rl", [128, NHEADS], F32)
        sp_ab = sbt("sp_ab", [128, NHEADS], F32)
        sp_v = sbt("sp_v", [128, NHEADS], F32)
        sp_t = sbt("sp_t", [128, NHEADS], F32)
        sp_s = sbt("sp_s", [128, NHEADS], F32)
        sp_w = sbt("sp_w", [128, NHEADS], F32)
        dtq_tm = sbt("dtq_tm", [128, NHEADS], BF16)
        ddq_tm = sbt("ddq_tm", [128, NHEADS], BF16)
        a_fm = sbt("a_fm", [32, 128], F32)
        dt_fm = sbt("dt_fm", [32, 128], F32)
        decay_fm = sbt("decay_fm", [32, 128], F32)
        ddq_fm = sbt("ddq_fm", [32, 128], BF16)
        a_hilo = sbt("a_hilo", [32, 256], BF16)
        nega = sbt("nega", [32, 256], BF16)
        a_tm = sbt("a_tm", [128, NHEADS], F32)
        ea_tm = sbt("ea_tm", [128, NHEADS], F32)
        cdec = sbt("cdec", [32, 1], F32)
        cdB = sbt("cdB", [128, NHEADS], F32)
        B_fm = sbt("B_fm", [128, 128], BF16)
        C_fm = sbt("C_fm", [128, 128], BF16)
        msc = sbt("msc", [128, 128], BF16)

        pbk = [st.enter_context(nc.psum_tensor(f"pb{i}", [128, 512], F32)).ap()
               for i in range(8)]

        SU = [Sem(st, nc, f"SU{i}") for i in range(2)]
        SW = [Sem(st, nc, f"SW{i}") for i in range(2)]
        SWC = Sem(st, nc, "SWC")
        SAF = [Sem(st, nc, f"SAF{i}") for i in range(2)]
        STL = Sem(st, nc, "STL")
        SO = Sem(st, nc, "SO")
        PE = Sem(st, nc, "PE")
        DV = Sem(st, nc, "DV")
        AC = Sem(st, nc, "AC")
        GP = Sem(st, nc, "GP")

        blk = st.enter_context(nc.Block())

        @blk.sync
        def _(eng):
            eng = R(eng)
            for dst, src in [(identb, identb_d), (identf, identf_d), (maskc, mask_d),
                             (e8c, e8_d), (sall, sall_d), (sbm, sbm_d),
                             (ones1b, ones1b_d), (ones1f, ones1f_d),
                             (cdec, alog_d), (nw_t, nw_d)]:
                SWC.dma(eng, dst, src)
            SU[0].dma(eng, ubuf[:, 0:D_MODEL], u_d[0, 0:Q, :])

        @blk.gpsimd
        def _(eng):
            eng = R(eng)
            eng.memset(ones32, 1.0)
            eng.memset(powm, -0.5)
            GP.bump(eng.last)
        INIT_GP_MEMS = GP.n

        @blk.scalar
        def _(eng):
            eng = R(eng)
            eng.wait_ge(SWC.h, SWC.n)
            AC.bump(eng.activation(out=tpow[0:32, :], in_=cdec, func=AF.Exp,
                                   bias=0.0, scale=1.0))

        @blk.vector
        def _(eng):
            eng = R(eng)
            eng.wait_ge(AC.h, AC.n)
            eng.tensor_scalar(out=negA, in0=tpow[0:32, :], scalar1=-1.0, scalar2=None,
                              op0=ALU.mult)
            DV.bump(eng.last)

        pieces = []
        for k in range(8):
            for p0 in range(0, DIN, 1024):
                w = min(1024, DIN - p0)
                pieces.append((winT_d[k * 128:(k + 1) * 128, p0:p0 + w],
                               wcomb[:, k * DIN + p0: k * DIN + p0 + w], None, w))
        for k in range(16):
            pieces.append((woutT_d[k * 128:(k + 1) * 128, :],
                           wout[:, k * 1024:(k + 1) * 1024], nw_t[:, k:k + 1], 1024))

        dv_done = {}
        for i, (src, dst, scale, w) in enumerate(pieces):
            par = i % 2
            half = carry[:, par * 1024: par * 1024 + 1024]

            @blk.sync
            def _(eng, i=i, par=par, half=half, src=src, w=w):
                eng = R(eng)
                if i >= 2:
                    eng.wait_ge(DV.h, dv_done[i - 2])
                SW[par].dma(eng, half[:, 0:w], src)

            @blk.vector
            def _(eng, i=i, par=par, half=half, dst=dst, scale=scale, w=w):
                eng = R(eng)
                eng.wait_ge(SW[par].h, SW[par].n)
                sc = 1.0 if scale is None else scale
                eng.tensor_scalar(out=dst, in0=half[:, 0:w], scalar1=sc,
                                  scalar2=None, op0=ALU.mult)
                dv_done[i] = DV.bump(eng.last)

        row_jobs = []
        for d in range(DCONV):
            for p0 in range(0, CONV_DIM, 512):
                w = min(512, CONV_DIM - p0)
                row_jobs.append((convwT_d[d:d + 1, p0:p0 + w], w, "bcast",
                                 wB[:, d * CONV_DIM + p0: d * CONV_DIM + p0 + w]))
        for p0 in range(0, CONV_DIM, 512):
            w = min(512, CONV_DIM - p0)
            row_jobs.append((convb_d[:, p0:p0 + w], w, "row_cast", convb_r[:, p0:p0 + w]))
        for p0 in range(0, 1024, 512):
            row_jobs.append((fbias_d[:, p0:p0 + 512], 512, "row_cast", fb_r[:, p0:p0 + 512]))
        row_jobs.append((dtb_d, 32, "dtb", None))
        row_jobs.append((dpar_d, 32, "bcast", DBt))

        rj_done = {}
        for i, (src, w, kind, dst) in enumerate(row_jobs):
            par = i % 2
            half = row_st[:, par * 512: par * 512 + 512]
            bank = pbk[par]

            @blk.sync
            def _(eng, i=i, par=par, half=half, src=src, w=w):
                eng = R(eng)
                if i >= 2:
                    eng.wait_ge(DV.h, rj_done[i - 2])
                SW[par].dma(eng, half[:, 0:w], src)

            if kind == "bcast":
                @blk.tensor
                def _(eng, i=i, par=par, half=half, bank=bank, w=w):
                    eng = R(eng)
                    eng.wait_ge(SW[par].h, SW[par].n)
                    eng.wait_ge(SWC.h, SWC.n)
                    if i >= 2:
                        eng.wait_ge(DV.h, rj_done[i - 2])
                    PE.bump(eng.matmul(bank[:, 0:w], ones1f, half[0:1, 0:w],
                                       start=True, stop=True))

                @blk.vector
                def _(eng, i=i, bank=bank, w=w, dst=dst):
                    eng = R(eng)
                    eng.wait_ge(PE.h, PE.n)
                    eng.tensor_scalar(out=dst, in0=bank[:, 0:w], scalar1=1.0,
                                      scalar2=None, op0=ALU.mult)
                    rj_done[i] = DV.bump(eng.last)
            elif kind == "row_cast":
                @blk.vector
                def _(eng, i=i, par=par, half=half, w=w, dst=dst):
                    eng = R(eng)
                    eng.wait_ge(SW[par].h, SW[par].n)
                    eng.tensor_scalar(out=dst, in0=half[0:1, 0:w], scalar1=1.0,
                                      scalar2=None, op0=ALU.mult)
                    rj_done[i] = DV.bump(eng.last)
            else:
                @blk.vector
                def _(eng, i=i, par=par, half=half):
                    eng = R(eng)
                    eng.wait_ge(SW[par].h, SW[par].n)
                    eng.tensor_scalar(out=dtb_r[:, 0:32], in0=half[0:1, 0:32],
                                      scalar1=1.0, scalar2=None, op0=ALU.mult)
                    eng.drain()
                    eng.tensor_tensor(out=dtb_r[:, 32:64], in0=half[0:1, 0:32],
                                      in1=dtb_r[:, 0:32], op=ALU.subtract)
                    rj_done[i] = DV.bump(eng.last)

        arep = carry[:, 0:128].rearrange("p (c b) -> p c b", c=4)
        brep = carry[:, 128:192].rearrange("p (c e) -> p c e", c=4)
        tmpf = carry[:, 256:512]
        accf = carry[:, 512:1024]

        ffn_jobs = []
        for (z1ap, z2ap, wdst) in [(z1_d, z2_d, wr), (z1i_d, z2i_d, wi)]:
            for mm in range(4):
                ffn_jobs.append((z1ap, z2ap, wdst, mm))

        fj_done = {}
        last_rj = max(rj_done.values()) if rj_done else 0
        for i, (z1ap, z2ap, wdst, mm) in enumerate(ffn_jobs):
            @blk.sync
            def _(eng, i=i, z1ap=z1ap, z2ap=z2ap, mm=mm):
                eng = R(eng)
                eng.wait_ge(DV.h, fj_done[i - 1] if i > 0 else last_rj)
                a_src = z1ap[mm * 8:(mm + 1) * 8][:, None, :, :].broadcast_to([8, 16, 4, 32])
                b_src = z2ap[None, :, :, :].broadcast_to([8, 16, 4, 16])
                SW[0].dma(eng, arep, a_src)
                SW[0].dma(eng, brep, b_src)

            @blk.vector
            def _(eng, i=i, wdst=wdst, mm=mm):
                eng = R(eng)
                eng.wait_ge(SW[0].h, SW[0].n)
                av = arep[:, :, :, None].broadcast_to([128, 4, 32, 16])
                bv = brep[:, :, None, :].broadcast_to([128, 4, 16, 16])
                accv = accf.rearrange("p (b e) -> p b e", b=32)
                tmpv = tmpf.rearrange("p (b e) -> p b e", b=16)
                eng.tensor_tensor(out=accv, in0=av[:, 0],
                                  in1=brep[:, 0, None, :].broadcast_to([128, 32, 16]),
                                  op=ALU.mult)
                for c in range(1, 4):
                    for hbk in range(2):
                        s0, s1 = hbk * 16, (hbk + 1) * 16
                        eng.drain()
                        eng.tensor_tensor(
                            out=tmpv, in0=av[:, c, s0:s1],
                            in1=brep[:, c, None, :].broadcast_to([128, 16, 16]),
                            op=ALU.mult)
                        eng.drain()
                        eng.tensor_tensor(out=accv[:, s0:s1], in0=accv[:, s0:s1],
                                          in1=tmpv, op=ALU.add)
                eng.drain()
                eng.tensor_scalar(out=wdst[:, mm * 512:(mm + 1) * 512], in0=accf,
                                  scalar1=1.0, scalar2=None, op0=ALU.mult)
                fj_done[i] = DV.bump(eng.last)

        @blk.vector
        def _(eng):
            eng = R(eng)
            eng.drain()
            eng.tensor_scalar(out=nwi, in0=wi, scalar1=-1.0, scalar2=None, op0=ALU.mult)
            DV.bump(eng.last)
        INIT_DV = DV.n
        INIT_GP = GP.n

        NCH = n_seq * n_chunks
        GRP = [(g * 512, min(512, DIN - g * 512)) for g in range(9)]

        fin_dv = {}
        ac_last = {}
        gfm_last = {}
        cu2_dv = {}
        cb_gp = {}
        yd_pe = {}
        st_pe = {}
        m1_pe = {}
        eexp_ac = {}
        xd_gp = {}
        sq_ac = {}
        stl_cnt = {}
        yo_pe_last = {}

        for ci in range(NCH):
            par = ci % 2
            k_in_seq = ci % n_chunks
            seq = ci // n_chunks
            first = (k_in_seq == 0)
            u_t = ubuf[:, par * D_MODEL:(par + 1) * D_MODEL]
            yd_pe[ci] = {}
            st_pe[ci] = {}
            m1_pe[ci] = {}
            eexp_ac[ci] = {}
            xd_gp[ci] = {}
            cu2_dv[ci] = {}
            cb_gp[ci] = {}
            stl_cnt[ci] = {}

            @blk.sync
            def _(eng, ci=ci):
                eng = R(eng)
                if ci + 1 < NCH:
                    p2 = (ci + 1) % 2
                    if ci >= 1:
                        eng.wait_ge(SO.h, 16 * ci)
                    s2, k2 = (ci + 1) // n_chunks, (ci + 1) % n_chunks
                    SU[p2].dma(eng, ubuf[:, p2 * D_MODEL:(p2 + 1) * D_MODEL],
                               u_d[s2, k2 * Q:(k2 + 1) * Q, :])

            uload_n = (ci // 2) + 1

            @blk.vector
            def _(eng, ci=ci, u_t=u_t, uload_n=uload_n):
                eng = R(eng)
                eng.wait_ge(SU[ci % 2].h, 16 * uload_n)
                if ci == 0:
                    eng.wait_ge(DV.h, INIT_DV)
                eng.bn_stats(stats[:, 0:6], u_t[:, 0:512])
                eng.bn_stats(stats[:, 6:12], u_t[:, 512:1024])
                eng.drain()
                eng.bn_aggr(mv, stats)
                DV.bump(eng.last)
            mv_dv = DV.n

            @blk.gpsimd
            def _(eng, ci=ci, first=first, mv_dv=mv_dv):
                eng = R(eng)
                if first:
                    if ci > 0:
                        eng.wait_ge(PE.h, yo_pe_last[ci - 1])
                        eng.wait_ge(DV.h, cu2_dv[ci - 1][1])
                    else:
                        eng.wait_ge(DV.h, INIT_DV)
                    eng.memset(carry, 0.0)
                    eng.memset(carry_b, 0.0)
                    eng.drain()
                eng.wait_ge(DV.h, mv_dv)
                eng.tensor_scalar(out=tpow, in0=mv[:, 1:2], scalar1=LNEPS,
                                  scalar2=None, op0=ALU.add)
                eng.drain()
                eng.tensor_tensor(out=rstd, in0=tpow, in1=powm, op=ALU.pow)
                GP.bump(eng.last)
            gp_rstd = GP.n

            @blk.vector
            def _(eng, u_t=u_t, gp_rstd=gp_rstd):
                eng = R(eng)
                eng.wait_ge(GP.h, gp_rstd)
                eng.tensor_scalar(out=xn_tm, in0=u_t, scalar1=mv[:, 0:1],
                                  scalar2=rstd, op0=ALU.subtract, op1=ALU.mult)
                DV.bump(eng.last)
            xn_dv = DV.n

            @blk.tensor
            def _(eng, xn_dv=xn_dv, ci=ci):
                eng = R(eng)
                eng.wait_ge(DV.h, xn_dv)
                b5 = pbk[5].bitcast(BF16)
                for t in range(8):
                    eng.matmul(b5[:, t * 128:(t + 1) * 128], xn_tm[:, t * 128:(t + 1) * 128],
                               identb, start=True, stop=True, is_transpose=True)
                PE.bump(eng.last)
            xnT_pe = PE.n

            @blk.vector
            def _(eng, xnT_pe=xnT_pe):
                eng = R(eng)
                eng.wait_ge(PE.h, xnT_pe)
                eng.tensor_copy(xn_fm, pbk[5].bitcast(BF16)[:, 0:1024])
                DV.bump(eng.last)
            xnfm_dv = DV.n

            @blk.tensor
            def _(eng, xnfm_dv=xnfm_dv, ci=ci):
                eng = R(eng)
                eng.wait_ge(DV.h, xnfm_dv)
                if ci >= 1:
                    eng.wait_ge(AC.h, ac_last[ci - 1])
                for kk in range(8):
                    lhs = xn_fm[:, kk * 128:(kk + 1) * 128]
                    for g in range(5):
                        c0, w = GRP[g]
                        eng.matmul(pbk[g][:, 0:w], lhs,
                                   wcomb[:, kk * DIN + c0: kk * DIN + c0 + w],
                                   start=(kk == 0), stop=(kk == 7))
                PE.bump(eng.last)
            pass0_pe = PE.n

            dv_prev_z = None
            for g in range(4):
                @blk.scalar
                def _(eng, g=g, p=pass0_pe, ci=ci, dv_prev_z=dv_prev_z):
                    eng = R(eng)
                    if g == 0:
                        eng.wait_ge(PE.h, p)
                        if ci >= 1:
                            eng.wait_ge(DV.h, fin_dv[ci - 1])
                    else:
                        eng.wait_ge(DV.h, dv_prev_z)
                    eng.activation(out=t_rot, in_=pbk[g], func=AF.Tanh,
                                   bias=0.0, scale=0.5)
                    AC.bump(eng.last)
                ac_zg = AC.n

                @blk.vector
                def _(eng, g=g, ac_zg=ac_zg):
                    eng = R(eng)
                    eng.wait_ge(AC.h, ac_zg)
                    eng.scalar_tensor_tensor(
                        out=sz2[:, g * 512:(g + 1) * 512],
                        in0=t_rot, scalar=1.0,
                        in1=pbk[g], op0=ALU.add, op1=ALU.mult)
                    DV.bump(eng.last)
                dv_prev_z = DV.n

            @blk.vector
            def _(eng):
                eng = R(eng)
                eng.tensor_copy(xBC[:, 0:512], pbk[4])
                DV.bump(eng.last)
            evac0_dv = DV.n

            @blk.tensor
            def _(eng, evac0_dv=evac0_dv, ci=ci):
                eng = R(eng)
                eng.wait_ge(DV.h, evac0_dv)
                if ci >= 1:
                    eng.wait_ge(DV.h, cu2_dv[ci - 1][1])
                for kk in range(8):
                    lhs = xn_fm[:, kk * 128:(kk + 1) * 128]
                    for g in range(5, 9):
                        c0, w = GRP[g]
                        eng.matmul(pbk[g % 5][:, 0:w], lhs,
                                   wcomb[:, kk * DIN + c0: kk * DIN + c0 + w],
                                   start=(kk == 0), stop=(kk == 7 and g != 8))
                eng.matmul(pbk[3][:, 256:288], ones1b, dtb_r[:, 0:32],
                           start=False, stop=False)
                eng.matmul(pbk[3][:, 256:288], ones1b, dtb_r[:, 32:64],
                           start=False, stop=True)
                for kk in range(8):
                    lhs = xn_fm[:, kk * 128:(kk + 1) * 128]
                    rr = wr if kk < 4 else nwi
                    ii = wi if kk < 4 else wr
                    kkm = kk % 4
                    eng.matmul(pbk[5][:, 0:512], lhs, rr[:, kkm * 512:(kkm + 1) * 512],
                               start=(kk == 0), stop=False)
                    eng.matmul(pbk[6][:, 0:512], lhs, ii[:, kkm * 512:(kkm + 1) * 512],
                               start=(kk == 0), stop=False)
                eng.matmul(pbk[5][:, 0:512], ones1b, fb_r[:, 0:512], start=False, stop=True)
                eng.matmul(pbk[6][:, 0:512], ones1b, fb_r[:, 512:1024], start=False, stop=True)
                PE.bump(eng.last)
            pass1_pe = PE.n

            @blk.vector
            def _(eng, p=pass1_pe, u_t=u_t):
                eng = R(eng)
                eng.wait_ge(PE.h, p)
                for g in range(5, 8):
                    c0, w = GRP[g]
                    eng.tensor_copy(xBC[:, c0 - 2048: c0 - 2048 + w], pbk[g % 5][:, 0:w])
                eng.tensor_copy(xBC[:, 2048:2304], pbk[3][:, 0:256])
                eng.drain()
                eng.tensor_tensor(out=u_t[:, 0:512], in0=u_t[:, 0:512], in1=pbk[5],
                                  op=ALU.add)
                eng.tensor_tensor(out=u_t[:, 512:1024], in0=u_t[:, 512:1024], in1=pbk[6],
                                  op=ALU.add)
                DV.bump(eng.last)
            evac1_dv = DV.n

            @blk.scalar
            def _(eng, p=pass1_pe):
                eng = R(eng)
                eng.wait_ge(PE.h, p)
                dtp = pbk[3][:, 256:288]
                eng.activation(out=sp_rl, in_=dtp, func=AF.Relu, bias=0.0, scale=1.0)
                eng.activation(out=sp_ab, in_=dtp, func=AF.Abs, bias=0.0, scale=1.0)
                eng.drain()
                eng.activation(out=sp_v, in_=sp_ab, func=AF.Exp, bias=0.0, scale=1.0)
                AC.bump(eng.last)
            ac_dt = AC.n

            @blk.vector
            def _(eng, ac_dt=ac_dt):
                eng = R(eng)
                eng.wait_ge(AC.h, ac_dt)
                eng.tensor_scalar(out=sp_t, in0=sp_v, scalar1=2.0, scalar2=1.0,
                                  op0=ALU.mult, op1=ALU.add)
                eng.drain()
                eng.reciprocal(sp_t, sp_t)
                eng.drain()
                eng.tensor_tensor(out=sp_s, in0=sp_t, in1=sp_t, op=ALU.mult)
                eng.drain()
                eng.tensor_scalar(out=sp_w, in0=sp_s, scalar1=1.0 / 7, scalar2=1.0 / 5,
                                  op0=ALU.mult, op1=ALU.add)
                eng.drain()
                eng.scalar_tensor_tensor(out=sp_w, in0=sp_w, scalar=0.0, in1=sp_s,
                                         op0=ALU.add, op1=ALU.mult)
                eng.drain()
                eng.tensor_scalar(out=sp_w, in0=sp_w, scalar1=1.0 / 3, scalar2=None,
                                  op0=ALU.add)
                eng.drain()
                eng.scalar_tensor_tensor(out=sp_w, in0=sp_w, scalar=0.0, in1=sp_s,
                                         op0=ALU.add, op1=ALU.mult)
                eng.drain()
                eng.tensor_scalar(out=sp_w, in0=sp_w, scalar1=1.0, scalar2=None,
                                  op0=ALU.add)
                eng.drain()
                eng.tensor_tensor(out=sp_w, in0=sp_w, in1=sp_t, op=ALU.mult)
                eng.drain()
                eng.scalar_tensor_tensor(out=dt_tm, in0=sp_w, scalar=2.0, in1=sp_rl,
                                         op0=ALU.mult, op1=ALU.add)
                eng.drain()
                eng.tensor_scalar(out=dtq_tm, in0=dt_tm, scalar1=0.25, scalar2=None,
                                  op0=ALU.mult)
                DV.bump(eng.last)
            dt_dv = DV.n

            @blk.tensor
            def _(eng, dt_dv=dt_dv):
                eng = R(eng)
                eng.wait_ge(DV.h, dt_dv)
                eng.matmul(pbk[7][0:32, 0:128], dt_tm[:, 0:32], identf,
                           start=True, stop=True, is_transpose=True)
                PE.bump(eng.last)
            dtT_pe = PE.n

            @blk.vector
            def _(eng, dtT_pe=dtT_pe):
                eng = R(eng)
                eng.wait_ge(PE.h, dtT_pe)
                dtfmp = pbk[7][0:32, 0:128]
                eng.tensor_copy(dt_fm, dtfmp)
                eng.tensor_scalar(out=decay_fm, in0=dtfmp, scalar1=negA, scalar2=None,
                                  op0=ALU.mult)
                eng.drain()
                eng.tensor_tensor_scan(out=a_fm, data0=ones32, data1=decay_fm,
                                       initial=0.0, op0=ALU.mult, op1=ALU.add)
                eng.drain()
                eng.tensor_scalar(out=a_hilo[:, 0:128], in0=a_fm, scalar1=1.0,
                                  scalar2=None, op0=ALU.mult)
                eng.drain()
                eng.tensor_tensor(out=a_hilo[:, 128:256], in0=a_fm, in1=a_hilo[:, 0:128],
                                  op=ALU.subtract)
                eng.drain()
                eng.tensor_scalar(out=nega, in0=a_hilo, scalar1=-1.0, scalar2=None,
                                  op0=ALU.mult)
                DV.bump(eng.last)
            a_dv = DV.n

            @blk.tensor
            def _(eng, a_dv=a_dv):
                eng = R(eng)
                eng.wait_ge(DV.h, a_dv)
                eng.matmul(pbk[7][:, 128:160], a_fm, identf[0:32, 0:32],
                           start=True, stop=True, is_transpose=True)
                PE.bump(eng.last)
            aT_pe = PE.n

            @blk.vector
            def _(eng, aT_pe=aT_pe):
                eng = R(eng)
                eng.wait_ge(PE.h, aT_pe)
                eng.tensor_copy(a_tm, pbk[7][:, 128:160])
                DV.bump(eng.last)
            atm_dv = DV.n

            @blk.scalar
            def _(eng, atm_dv=atm_dv):
                eng = R(eng)
                eng.wait_ge(DV.h, atm_dv)
                eng.activation(out=ea_tm, in_=a_tm, func=AF.Exp, bias=0.0, scale=1.0)
                eng.activation(out=decay_fm, in_=a_fm, func=AF.Exp,
                               bias=a_fm[:, 127:128], scale=-1.0)
                eng.activation(out=cdec, in_=a_fm[:, 127:128], func=AF.Exp,
                               bias=0.0, scale=1.0)
                AC.bump(eng.last)
            ac_ea = AC.n

            @blk.vector
            def _(eng, ac_ea=ac_ea):
                eng = R(eng)
                eng.wait_ge(AC.h, ac_ea)
                eng.scalar_tensor_tensor(out=ddq_fm, in0=decay_fm, scalar=0.25,
                                         in1=dt_fm, op0=ALU.mult, op1=ALU.mult)
                DV.bump(eng.last)
            ddqfm_dv = DV.n

            @blk.tensor
            def _(eng, ddqfm_dv=ddqfm_dv):
                eng = R(eng)
                eng.wait_ge(DV.h, ddqfm_dv)
                b7h = pbk[7].bitcast(BF16)
                eng.matmul(b7h[0:128, 512:544], ddq_fm, identb[0:32, 0:32],
                           start=True, stop=True, is_transpose=True)
                eng.matmul(pbk[7][0:1, 160:192], cdec, identf[0:32, 0:32],
                           start=True, stop=True, is_transpose=True)
                PE.bump(eng.last)
            ddqT_pe = PE.n

            @blk.vector
            def _(eng, ddqT_pe=ddqT_pe):
                eng = R(eng)
                eng.wait_ge(PE.h, ddqT_pe)
                eng.tensor_copy(ddq_tm, pbk[7].bitcast(BF16)[:, 512:544])
                eng.tensor_copy(cdec_r, pbk[7][0:1, 160:192])
                DV.bump(eng.last)
            cdecr_dv = DV.n

            @blk.tensor
            def _(eng, cdecr_dv=cdecr_dv):
                eng = R(eng)
                eng.wait_ge(DV.h, cdecr_dv)
                eng.matmul(pbk[7][:, 192:224], ones1f, cdec_r[:, 0:32],
                           start=True, stop=True)
                PE.bump(eng.last)
            cdB_pe = PE.n

            @blk.vector
            def _(eng, cdB_pe=cdB_pe, ci=ci):
                eng = R(eng)
                eng.wait_ge(PE.h, cdB_pe)
                if ci >= 1:
                    eng.wait_ge(GP.h, cb_gp[ci - 1][1])
                eng.tensor_copy(cdB, pbk[7][:, 192:224])
                DV.bump(eng.last)
            cdB_dv = DV.n

            if ci == 0:
                STL_total_prev = {}
            if not first:
                @blk.tensor
                def _(eng, ci=ci, evac1_dv=evac1_dv, ac_dt=ac_dt):
                    eng = R(eng)
                    eng.wait_ge(DV.h, evac1_dv)
                    eng.wait_ge(AC.h, ac_dt)
                    eng.wait_ge(STL.h, 16 * STL_total_prev[ci])
                    for g in range(5):
                        c0 = g * 512
                        w = min(512, CONV_DIM - c0)
                        eng.matmul(pbk[g][:, 0:w], sbm, tails[:, c0:c0 + w],
                                   start=True, stop=False)
                    PE.bump(eng.last)
            bnd_pe = PE.n

            xw_dv = {}
            cnv_pe = {}
            for d in range(DCONV):
                @blk.vector
                def _(eng, d=d, ci=ci, evac1_dv=evac1_dv):
                    eng = R(eng)
                    eng.wait_ge(DV.h, evac1_dv)
                    if d == 0:
                        if ci >= 1:
                            eng.wait_ge(AC.h, sq_ac[ci - 1])
                    else:
                        eng.wait_ge(PE.h, cnv_pe[d - 1])
                        eng.wait_ge(STL.h, 16 * stl_cnt[ci][d - 1])
                    eng.tensor_tensor(out=xw, in0=xBC,
                                      in1=wB[:, d * CONV_DIM:(d + 1) * CONV_DIM],
                                      op=ALU.mult)
                    DV.bump(eng.last)
                xw_dv[d] = DV.n

                @blk.tensor
                def _(eng, d=d, xwdv=xw_dv[d], first=first, evac1_dv=evac1_dv,
                      ac_dt=ac_dt):
                    eng = R(eng)
                    eng.wait_ge(DV.h, xwdv)
                    if d == 0 and first:
                        eng.wait_ge(DV.h, evac1_dv)
                        eng.wait_ge(AC.h, ac_dt)
                    lhs = sall[:, d * 128:(d + 1) * 128]
                    for g in range(5):
                        c0 = g * 512
                        w = min(512, CONV_DIM - c0)
                        eng.matmul(pbk[g][:, 0:w], lhs, xw[:, c0:c0 + w],
                                   start=(d == 0 and first), stop=False)
                    PE.bump(eng.last)
                cnv_pe[d] = PE.n

                @blk.sync
                def _(eng, d=d, ci=ci, p=cnv_pe[d]):
                    eng = R(eng)
                    eng.wait_ge(PE.h, p)
                    STL.dma(eng, tails[d * 3:(d + 1) * 3, :], xw[125:128, :])
                stl_cnt[ci][d] = STL.n // 16
            STL_total_prev[ci + 1] = STL.n // 16

            @blk.tensor
            def _(eng):
                eng = R(eng)
                for g in range(5):
                    c0 = g * 512
                    w = min(512, CONV_DIM - c0)
                    eng.matmul(pbk[g][:, 0:w], ones1b, convb_r[:, c0:c0 + w],
                               start=False, stop=True)
                PE.bump(eng.last)
            conv_pe = PE.n

            dv_prev_c = None
            dv_cg = {}
            for g in range(5):
                w = min(512, CONV_DIM - g * 512)
                @blk.scalar
                def _(eng, g=g, w=w, conv_pe=conv_pe, dv_prev_c=dv_prev_c):
                    eng = R(eng)
                    if g == 0:
                        eng.wait_ge(PE.h, conv_pe)
                    else:
                        eng.wait_ge(DV.h, dv_prev_c)
                    eng.activation(out=t_rot[:, 0:w], in_=pbk[g][:, 0:w],
                                   func=AF.Tanh, bias=0.0, scale=0.5)
                    AC.bump(eng.last)
                ac_cg = AC.n

                @blk.vector
                def _(eng, g=g, w=w, ac_cg=ac_cg):
                    eng = R(eng)
                    eng.wait_ge(AC.h, ac_cg)
                    eng.scalar_tensor_tensor(
                        out=xBC[:, g * 512:g * 512 + w],
                        in0=t_rot[:, 0:w], scalar=1.0,
                        in1=pbk[g][:, 0:w], op0=ALU.add, op1=ALU.mult)
                    DV.bump(eng.last)
                dv_prev_c = DV.n
                dv_cg[g] = DV.n
            dv_c01 = dv_cg[1]
            dv_c23 = dv_cg[3]
            ac_c01 = None
            ac_c23 = None
            xbca_dv = DV.n

            @blk.tensor
            def _(eng, xbca_dv=xbca_dv):
                eng = R(eng)
                eng.wait_ge(DV.h, xbca_dv)
                b7h = pbk[7].bitcast(BF16)
                eng.matmul(b7h[:, 0:128], xBC[:, 2048:2176], identb,
                           start=True, stop=True, is_transpose=True)
                eng.matmul(b7h[:, 128:256], xBC[:, 2176:2304], identb,
                           start=True, stop=True, is_transpose=True)
                PE.bump(eng.last)
            bcT_pe = PE.n

            @blk.vector
            def _(eng, bcT_pe=bcT_pe):
                eng = R(eng)
                eng.wait_ge(PE.h, bcT_pe)
                b7h = pbk[7].bitcast(BF16)
                eng.tensor_copy(B_fm, b7h[:, 0:128])
                eng.tensor_copy(C_fm, b7h[:, 128:256])
                DV.bump(eng.last)
            bc_dv = DV.n

            @blk.tensor
            def _(eng, bc_dv=bc_dv):
                eng = R(eng)
                eng.wait_ge(DV.h, bc_dv)
                eng.matmul(pbk[6][:, 0:128], B_fm, C_fm, start=True, stop=True)
                PE.bump(eng.last)
            sc_pe = PE.n

            @blk.vector
            def _(eng, sc_pe=sc_pe):
                eng = R(eng)
                eng.wait_ge(PE.h, sc_pe)
                eng.tensor_copy(msc, pbk[6][:, 0:128])
                DV.bump(eng.last)
            msc_dv = DV.n

            yo_done_dv = {}
            yoMM_pe = {}
            for half in range(2):
                @blk.tensor
                def _(eng, half=half, ci=ci, first=first, msc_dv=msc_dv,
                      gp_rstd=gp_rstd):
                    eng = R(eng)
                    if half == 0:
                        eng.wait_ge(DV.h, msc_dv)
                    else:
                        eng.wait_ge(DV.h, cu2_dv[ci][0])
                    if first:
                        eng.wait_ge(GP.h, gp_rstd)
                    else:
                        eng.wait_ge(GP.h, cb_gp[ci - 1][half])
                    for qq in range(2):
                        eng.matmul(pbk[6 + qq][:, 0:512], C_fm,
                                   carry_b[:, half * 1024 + qq * 512:
                                           half * 1024 + (qq + 1) * 512],
                                   start=True, stop=True)
                    PE.bump(eng.last)
                yoMM_pe[half] = PE.n

                for qq in range(2):
                    q = half * 2 + qq
                    qp = q % 2
                    pb0 = pbk[0 + 2 * qp]
                    pb1 = pbk[1 + 2 * qp]
                    sldb = sld[:, qp * 1024:(qp + 1) * 1024]
                    dtxb = dtx[:, qp * 512:(qp + 1) * 512]

                    @blk.sync
                    def _(eng, q=q, qp=qp, ci=ci, a_dv=a_dv):
                        eng = R(eng)
                        eng.wait_ge(DV.h, a_dv)
                        if q >= 2:
                            eng.wait_ge(PE.h, m1_pe[ci][q - 2])
                        elif ci >= 1:
                            eng.wait_ge(PE.h, m1_pe[ci - 1][q + 2])
                        dst = aflat[:, qp * 2048:(qp + 1) * 2048].rearrange(
                            "p (r c) -> p r c", r=8)
                        SAF[qp].dma(eng, dst, a_hilo[8 * q:8 * (q + 1), :])
                    saf_n = SAF[qp].n

                    @blk.tensor
                    def _(eng, q=q, qp=qp, ci=ci, pb0=pb0, pb1=pb1, saf_n=saf_n,
                          dv_c01=dv_c01, dv_c23=dv_c23):
                        eng = R(eng)
                        eng.wait_ge(SAF[qp].h, 16 * (saf_n // 16))
                        if q >= 2:
                            eng.wait_ge(AC.h, eexp_ac[ci][q - 2])
                        else:
                            eng.wait_ge(DV.h, dv_c01 if qp == 0 else dv_c23)
                        flv = aflat[:, qp * 2048:(qp + 1) * 2048].rearrange(
                            "p (r c) -> p r c", r=8)
                        for ih, pbx in ((0, pb0), (1, pb1)):
                            eng.matmul(pbx.rearrange("p (h i) -> p h i", h=4),
                                       ones1b, flv[:, 4 * ih:4 * (ih + 1), 0:128],
                                       start=True, stop=False)
                            eng.matmul(pbx.rearrange("p (h i) -> p h i", h=4),
                                       ones1b, flv[:, 4 * ih:4 * (ih + 1), 128:256],
                                       start=False, stop=False)
                            e8v = e8c[:, q * 8:(q + 1) * 8][:, :, None].broadcast_to(
                                [32, 8, 128])
                            eng.matmul(pbx.rearrange("p (h i) -> p h i", h=4),
                                       nega[:, 0:128], e8v[:, 4 * ih:4 * (ih + 1), :],
                                       start=False, stop=False)
                            eng.matmul(pbx.rearrange("p (h i) -> p h i", h=4),
                                       nega[:, 128:256], e8v[:, 4 * ih:4 * (ih + 1), :],
                                       start=False, stop=False)
                            mvv = maskc[:, None, :].broadcast_to([128, 4, 128])
                            eng.matmul(pbx.rearrange("p (h i) -> p h i", h=4),
                                       identb, mvv, start=False, stop=True)
                        m1_pe[ci][q] = PE.bump(eng.last)

                    @blk.scalar
                    def _(eng, q=q, ci=ci, pb0=pb0, pb1=pb1, sldb=sldb):
                        eng = R(eng)
                        eng.wait_ge(PE.h, m1_pe[ci][q])
                        if q >= 2:
                            eng.wait_ge(PE.h, yd_pe[ci][q - 2])
                        elif ci >= 1:
                            eng.wait_ge(PE.h, yd_pe[ci - 1][q + 2])
                        eng.activation(out=sldb[:, 0:512], in_=pb0, func=AF.Exp,
                                       bias=0.0, scale=1.0)
                        eng.activation(out=sldb[:, 512:1024], in_=pb1, func=AF.Exp,
                                       bias=0.0, scale=1.0)
                        eexp_ac[ci][q] = AC.bump(eng.last)

                    @blk.gpsimd
                    def _(eng, q=q, ci=ci, xbca_dv=xbca_dv):
                        eng = R(eng)
                        eng.wait_ge(DV.h, xbca_dv)
                        if q >= 1:
                            eng.wait_ge(PE.h, yd_pe[ci][q - 1])
                        elif ci >= 1:
                            eng.wait_ge(PE.h, yd_pe[ci - 1][3])
                        xv = xBC[:, 0:2048].rearrange("p (h x) -> p h x", h=32)[
                            :, 8 * q:8 * (q + 1), :]
                        dv_ = DBt[:, 8 * q:8 * (q + 1), None].broadcast_to([128, 8, 64])
                        eng.tensor_tensor(out=xDq.rearrange("p (h x) -> p h x", h=8),
                                          in0=xv, in1=dv_, op=ALU.mult)
                        xd_gp[ci][q] = GP.bump(eng.last)

                    @blk.vector
                    def _(eng, q=q, qp=qp, ci=ci, sldb=sldb, dtxb=dtxb, dt_dv=dt_dv,
                          msc_dv=msc_dv):
                        eng = R(eng)
                        eng.wait_ge(AC.h, eexp_ac[ci][q])
                        sv = sldb.rearrange("p (h i) -> p h i", h=8)
                        eng.tensor_tensor(out=sv, in0=sv,
                                          in1=msc[:, None, :].broadcast_to([128, 8, 128]),
                                          op=ALU.mult)
                        if q >= 2:
                            eng.wait_ge(PE.h, st_pe[ci][q - 2])
                        elif ci >= 1:
                            eng.wait_ge(PE.h, st_pe[ci - 1][q + 2])
                        eng.drain()
                        xv = xBC[:, 0:2048].rearrange("p (h x) -> p h x", h=32)[
                            :, 8 * q:8 * (q + 1), :]
                        eng.tensor_tensor(out=dtxb.rearrange("p (h x) -> p h x", h=8),
                                          in0=xv,
                                          in1=dtq_tm[:, 8 * q:8 * (q + 1), None
                                                     ].broadcast_to([128, 8, 64]),
                                          op=ALU.mult)
                        DV.bump(eng.last)
                    sld_dv = DV.n

                    @blk.tensor
                    def _(eng, q=q, qp=qp, ci=ci, sld_dv=sld_dv, sldb=sldb, dtxb=dtxb,
                          half=half):
                        eng = R(eng)
                        eng.wait_ge(DV.h, sld_dv)
                        eng.wait_ge(GP.h, xd_gp[ci][q])
                        ydb = pbk[4 + qp]
                        for hh in range(8):
                            eng.matmul(ydb[:, hh * 64:(hh + 1) * 64],
                                       sldb[:, hh * 128:(hh + 1) * 128],
                                       dtxb[:, hh * 64:(hh + 1) * 64],
                                       start=(hh == 0), stop=False)
                        eng.matmul(ydb[:, 0:512], identb, xDq, start=False, stop=True)
                        yd_pe[ci][q] = PE.bump(eng.last)

                for qq in range(2):
                    q = half * 2 + qq
                    @blk.vector
                    def _(eng, q=q, qq=qq, ci=ci, half=half):
                        eng = R(eng)
                        eng.wait_ge(PE.h, yd_pe[ci][q])
                        eng.wait_ge(PE.h, yoMM_pe[half])
                        eng.drain()
                        yob = pbk[6 + qq]
                        eav = ea_tm[:, 8 * q:8 * (q + 1), None].broadcast_to([128, 8, 64])
                        eng.tensor_tensor(out=yo.rearrange("p (h x) -> p h x", h=8),
                                          in0=yob.rearrange("p (h x) -> p h x", h=8),
                                          in1=eav, op=ALU.mult)
                        eng.drain()
                        eng.tensor_tensor(out=yo, in0=yo, in1=pbk[4 + (q % 2)],
                                          op=ALU.add)
                        eng.drain()
                        gq = sz2[:, q * 512:(q + 1) * 512]
                        eng.tensor_tensor(out=gq, in0=yo, in1=gq, op=ALU.mult)
                        yo_done_dv[q] = DV.bump(eng.last)

                for qq in range(2):
                    q = half * 2 + qq
                    qp = q % 2

                    @blk.vector
                    def _(eng, q=q, qp=qp, ci=ci):
                        eng = R(eng)
                        eng.wait_ge(PE.h, yd_pe[ci][q])
                        dtxb = dtx[:, qp * 512:(qp + 1) * 512]
                        xv = xBC[:, 0:2048].rearrange("p (h x) -> p h x", h=32)[
                            :, 8 * q:8 * (q + 1), :]
                        eng.tensor_tensor(out=dtxb.rearrange("p (h x) -> p h x", h=8),
                                          in0=xv,
                                          in1=ddq_tm[:, 8 * q:8 * (q + 1), None
                                                     ].broadcast_to([128, 8, 64]),
                                          op=ALU.mult)
                        DV.bump(eng.last)
                    ddtx_dv = DV.n

                    @blk.tensor
                    def _(eng, q=q, qq=qq, qp=qp, ci=ci, ddtx_dv=ddtx_dv):
                        eng = R(eng)
                        eng.wait_ge(DV.h, ddtx_dv)
                        eng.wait_ge(DV.h, yo_done_dv[q])
                        eng.matmul(pbk[6 + qq][:, 0:512], xBC[:, 2048:2176],
                                   dtx[:, qp * 512:(qp + 1) * 512],
                                   start=True, stop=True)
                        st_pe[ci][q] = PE.bump(eng.last)

                @blk.vector
                def _(eng, half=half, ci=ci):
                    eng = R(eng)
                    eng.wait_ge(PE.h, st_pe[ci][half * 2 + 1])
                    ch = carry[:, half * 1024:(half + 1) * 1024]
                    cv = cdB[:, 16 * half:16 * (half + 1), None].broadcast_to(
                        [128, 16, 64])
                    eng.tensor_tensor(out=ch.rearrange("p (h x) -> p h x", h=16),
                                      in0=ch.rearrange("p (h x) -> p h x", h=16),
                                      in1=cv, op=ALU.mult)
                    eng.drain()
                    eng.tensor_tensor(out=ch[:, 0:512], in0=ch[:, 0:512], in1=pbk[6],
                                      op=ALU.add)
                    eng.tensor_tensor(out=ch[:, 512:1024], in0=ch[:, 512:1024],
                                      in1=pbk[7], op=ALU.add)
                    cu2_dv[ci][half] = DV.bump(eng.last)

                @blk.gpsimd
                def _(eng, half=half, ci=ci):
                    eng = R(eng)
                    eng.wait_ge(DV.h, cu2_dv[ci][half])
                    ch = carry[:, half * 1024:(half + 1) * 1024]
                    eng.tensor_scalar(out=carry_b[:, half * 1024:(half + 1) * 1024],
                                      in0=ch, scalar1=1.0, scalar2=None, op0=ALU.mult)
                    cb_gp[ci][half] = GP.bump(eng.last)

            g_all_dv = max(yo_done_dv.values())
            yo_pe_last[ci] = yoMM_pe[1]

            @blk.scalar
            def _(eng, ci=ci, g_all_dv=g_all_dv):
                eng = R(eng)
                eng.wait_ge(DV.h, g_all_dv)
                eng.wait_ge(STL.h, 16 * STL_total_prev[ci + 1])
                eng.activation(out=xw[:, 0:2048], in_=sz2, func=AF.Square,
                               bias=0.0, scale=1.0, accum_out=ssum)
                sq_ac[ci] = AC.bump(eng.last)
            ac_last[ci] = AC.n

            @blk.gpsimd
            def _(eng, ci=ci):
                eng = R(eng)
                eng.wait_ge(AC.h, sq_ac[ci])
                eng.tensor_scalar(out=tpow, in0=ssum, scalar1=1.0 / D_INNER,
                                  scalar2=RMSEPS, op0=ALU.mult, op1=ALU.add)
                eng.drain()
                eng.tensor_tensor(out=rstd2, in0=tpow, in1=powm, op=ALU.pow)
                GP.bump(eng.last)
            rms_gp = GP.n

            for bt in range(4):
                @blk.tensor
                def _(eng, bt=bt, ci=ci, g_all_dv=g_all_dv):
                    eng = R(eng)
                    if bt == 0:
                        eng.wait_ge(DV.h, g_all_dv)
                        eng.wait_ge(AC.h, eexp_ac[ci][3])
                    b01 = pbk[bt % 2].bitcast(BF16)
                    for t in range(4):
                        tt_ = bt * 4 + t
                        eng.matmul(b01[:, t * 128:(t + 1) * 128],
                                   sz2[:, tt_ * 128:(tt_ + 1) * 128],
                                   identb, start=True, stop=True, is_transpose=True)
                    PE.bump(eng.last)
                gT_pe = PE.n

                @blk.vector
                def _(eng, bt=bt, gT_pe=gT_pe):
                    eng = R(eng)
                    eng.wait_ge(PE.h, gT_pe)
                    eng.tensor_copy(gfm[:, (bt % 2) * 512:(bt % 2 + 1) * 512],
                                    pbk[bt % 2].bitcast(BF16)[:, 0:512])
                    DV.bump(eng.last)
                gfm_dv = DV.n

                @blk.tensor
                def _(eng, bt=bt, ci=ci, gfm_dv=gfm_dv):
                    eng = R(eng)
                    eng.wait_ge(DV.h, gfm_dv)
                    for t in range(4):
                        kk = bt * 4 + t
                        lhs = gfm[:, (bt % 2) * 512 + t * 128:(bt % 2) * 512 + (t + 1) * 128]
                        for e in range(2):
                            eng.matmul(pbk[4 + e][:, 0:512], lhs,
                                       wout[:, kk * 1024 + e * 512:
                                            kk * 1024 + e * 512 + 512],
                                       start=(bt == 0 and t == 0),
                                       stop=(bt == 3 and t == 3))
                    PE.bump(eng.last)
            op_pe = PE.n
            gfm_last[ci] = DV.n

            @blk.vector
            def _(eng, op_pe=op_pe, rms_gp=rms_gp, u_t=u_t):
                eng = R(eng)
                eng.wait_ge(PE.h, op_pe)
                eng.wait_ge(GP.h, rms_gp)
                for e in range(2):
                    eng.scalar_tensor_tensor(
                        out=u_t[:, e * 512:(e + 1) * 512],
                        in0=pbk[4 + e], scalar=rstd2,
                        in1=u_t[:, e * 512:(e + 1) * 512],
                        op0=ALU.mult, op1=ALU.add)
                fin_dv[ci] = DV.bump(eng.last)

            @blk.sync
            def _(eng, ci=ci, seq=seq, k=k_in_seq, u_t=u_t):
                eng = R(eng)
                eng.wait_ge(DV.h, fin_dv[ci])
                SO.dma(eng, out_d[seq, k * Q:(k + 1) * Q, :], u_t)

        @blk.sync
        def _(eng):
            eng = R(eng)
            eng.wait_ge(SO.h, SO.n)

    return nc


def _host_prep(inputs):
    W_in = np.asarray(inputs["L0_attn_W_in"])
    W_out = np.asarray(inputs["L0_attn_W_out"])
    prep = {}
    prep["winT"] = np.ascontiguousarray(W_in.T)
    prep["woutT"] = np.ascontiguousarray(W_out.T)
    prep["nw"] = np.ascontiguousarray(
        np.asarray(inputs["L0_attn_norm_w"]).reshape(16, 128).T)
    prep["convwT"] = np.ascontiguousarray(np.asarray(inputs["L0_attn_conv_w"]).T)
    prep["convb"] = np.ascontiguousarray(np.asarray(inputs["L0_attn_conv_b"]).reshape(1, -1))
    prep["dtb"] = np.ascontiguousarray(np.asarray(inputs["L0_attn_dt_bias"]).reshape(1, -1))
    prep["alog"] = np.ascontiguousarray(np.asarray(inputs["L0_attn_A_log"]).reshape(-1, 1))
    prep["dpar"] = np.ascontiguousarray(np.asarray(inputs["L0_attn_D_param"]).reshape(1, -1))
    prep["fbias"] = np.ascontiguousarray(np.concatenate(
        [np.asarray(inputs["L0_ffn_bias_real"]),
         np.asarray(inputs["L0_ffn_bias_imag"])]).reshape(1, -1))
    prep["z1"] = np.ascontiguousarray(
        np.transpose(np.asarray(inputs["L0_ffn_G1_R"])[0], (2, 1, 0)))
    prep["z2"] = np.ascontiguousarray(
        np.transpose(np.asarray(inputs["L0_ffn_G2_R"])[:, :, 0, :], (2, 0, 1)))
    prep["z1i"] = np.ascontiguousarray(
        np.transpose(np.asarray(inputs["L0_ffn_G1_I"])[0], (2, 1, 0)))
    prep["z2i"] = np.ascontiguousarray(
        np.transpose(np.asarray(inputs["L0_ffn_G2_I"])[:, :, 0, :], (2, 0, 1)))
    prep["identb"] = np.eye(128, dtype=BF)
    prep["identf"] = np.eye(128, dtype=np.float32)
    mask = np.zeros((128, 128), np.float32)
    for j in range(128):
        mask[j, :j] = NEG
    prep["mask"] = mask.astype(BF)
    e8 = np.zeros((32, 32), np.float32)
    for kq in range(32):
        e8[kq, kq] = 1.0
    prep["e8"] = e8.astype(BF)
    sall = np.zeros((128, 4 * 128), np.float32)
    for d in range(4):
        for t in range(128):
            j = t - 3 + d
            if j >= 0:
                sall[j, d * 128 + t] = 1.0
    prep["sall"] = sall.astype(BF)
    sbm = np.zeros((12, 128), np.float32)
    for d in range(4):
        for r in range(3):
            t = r - d
            if t >= 0:
                sbm[d * 3 + r, t] = 1.0
    prep["sbm"] = sbm.astype(BF)
    prep["ones1b"] = np.ones((1, 128), BF)
    prep["ones1f"] = np.ones((1, 128), np.float32)
    return prep


def kernel(**inputs):
    u = np.asarray(inputs["L0_attn_u"])
    prep = _host_prep(inputs)
    nc = build_nc(2, 32)
    in_maps = []
    for c in range(8):
        m = dict(prep)
        m["u"] = np.ascontiguousarray(u[2 * c:2 * c + 2])
        in_maps.append(m)
    res = run_bass_kernel_spmd(nc, in_maps, core_ids=list(range(8)))
    return np.concatenate([r["out"] for r in res.results], axis=0)
